# version 01
# baseline (speedup 1.0000x reference)
import numpy as np
from contextlib import ExitStack

import concourse.bass as bass
import concourse.mybir as mybir
from concourse.bass_utils import run_bass_kernel_spmd

F32 = mybir.dt.float32
F16 = mybir.dt.float16
I16 = mybir.dt.int16
U16 = mybir.dt.uint16
AF = mybir.ActivationFunctionType
ALU = mybir.AluOpType
AX = mybir.AxisListType

NCORES = 8
B, T, D, I = 16, 2048, 1024, 3072
BPC = B // NCORES
TOK = BPC * T
NTILE = TOK // 128
QMAX = 32767.0
EPS = 1e-5
SCALE_FLOOR = 1e-12
NEG_BIG = -3.0e38
POS_BIG = 3.0e38

ND_MA = 7
ND_DIFF = 2
NF = 128
TB = T // 128
IC = I // 128
HC = D // 128
CW = 2048
NT34 = IC * (TOK // CW)


def _sb(nc, es, name, shape, dt):
    return es.enter_context(nc.sbuf_tensor(name, list(shape), dt))


class Ring:

    def __init__(self, nc, name, nslots):
        self.sems = [nc.semaphore(f"{name}{i}").__enter__()
                     for i in range(nslots)]
        self.n = nslots

    def inc(self, instr, item, amount=16):
        instr.then_inc(self.sems[item % self.n], amount)

    def wait(self, h, item, per=16):
        h.wait_ge(self.sems[item % self.n], per * (item // self.n + 1))

    def wait_all(self, h, nitems, per=16):
        for s in range(self.n):
            cnt = ((nitems - 1 - s) // self.n + 1) if nitems > s else 0
            if cnt:
                h.wait_ge(self.sems[s], per * cnt)


class Eng:

    def __init__(self, nc, name):
        self.sem = nc.semaphore(name).__enter__()
        self.n = 0
        self.h = None
        self.emitting = False

    def start(self, h=None, emitting=False):
        self.n = 0
        self.h = h
        self.emitting = emitting

    def do(self, emit, chain=True):
        if self.emitting:
            if chain and self.n:
                self.h.wait_ge(self.sem, self.n)
            ins = emit()
            ins.then_inc(self.sem, 1)
        self.n += 1
        return self.n

    def raw(self, emit):
        if self.emitting:
            emit()

    def wait(self, sem, val):
        if self.emitting:
            self.h.wait_ge(sem, val)

    def weng(self, other, count):
        if self.emitting and count > 0:
            self.h.wait_ge(other.sem, count)

    def dma(self, emit, ring=None, item=0, sem=None, amount=16):
        if self.emitting:
            if self.n:
                self.h.wait_ge(self.sem, self.n)
            ins = emit()
            if ring is not None:
                ring.inc(ins, item, amount)
            elif sem is not None:
                ins.then_inc(sem, amount)

    def rwait(self, ring, item, per=16):
        if self.emitting:
            ring.wait(self.h, item, per)

    def rwait_all(self, ring, nitems, per=16):
        if self.emitting:
            ring.wait_all(self.h, nitems, per)


def _two_pass(nc, bodies):
    for _, E, fn in bodies:
        E.start(None, False)
        fn(E)
    with nc.Block() as blk:
        for name, E, fn in bodies:
            deco = getattr(blk, name)

            def make(E=E, fn=fn):
                def _(h):
                    E.start(h, True)
                    fn(E)
                return _
            deco(make())


def _absmax_col(E, mx_t, mn_t, col_ap):
    E.do(lambda: E.h.tensor_scalar(out=col_ap, in0=mn_t[:], scalar1=-1.0,
                                   scalar2=None, op0=ALU.mult))
    return E.do(lambda: E.h.tensor_tensor(out=col_ap, in0=col_ap, in1=mx_t[:],
                                          op=ALU.max))



def build_L1():
    nc = bass.Bass("TRN2", target_bir_lowering=False, debug=False)
    x_in = nc.dram_tensor("x", [TOK, D], F32, kind="ExternalInput").ap()
    wup_in = nc.dram_tensor("wup", [I, D], F32, kind="ExternalInput").ap()
    wg_in = nc.dram_tensor("wg", [I, D], F32, kind="ExternalInput").ap()
    xn_out = nc.dram_tensor("xn16", [TOK, D], F16, kind="ExternalOutput").ap()
    wupq_out = nc.dram_tensor("wupq", [I, D], F16, kind="ExternalOutput").ap()
    wgq_out = nc.dram_tensor("wgq", [I, D], F16, kind="ExternalOutput").ap()
    mx_out = nc.dram_tensor("mx1", [1, 1], F32, kind="ExternalOutput").ap()

    WT = 2 * IC

    es = ExitStack()
    with es:
        xt = _sb(nc, es, "xt", [128, NTILE * D], F32)
        xnst = [_sb(nc, es, f"xnst{i}", [128, D], F16) for i in range(2)]
        dump_v = _sb(nc, es, "dump_v", [128, D], F16)
        stats = _sb(nc, es, "stats", [128, 12], F32)
        pair2 = _sb(nc, es, "pair2", [128, 2], F32)
        mu = _sb(nc, es, "mu", [128, NTILE], F32)
        var = _sb(nc, es, "var", [128, NTILE], F32)
        mxw = _sb(nc, es, "mxw", [128, NTILE], F32)
        mnw = _sb(nc, es, "mnw", [128, NTILE], F32)
        rstd = _sb(nc, es, "rstd", [128, NTILE], F32)
        nbias = _sb(nc, es, "nbias", [128, NTILE], F32)
        atok = _sb(nc, es, "atok", [128, NTILE], F32)
        scr1 = _sb(nc, es, "scr1", [128, NTILE], F32)
        col_scr = _sb(nc, es, "col_scr", [128, 1], F32)
        m11 = _sb(nc, es, "m11", [1, 1], F32)
        wt_ = [_sb(nc, es, f"wt{i}", [128, D], F32) for i in range(2)]
        wst = [_sb(nc, es, f"wst{i}", [128, D], F16) for i in range(2)]
        wsc = _sb(nc, es, "wsc", [128, D // 32], F32)
        wrc = _sb(nc, es, "wrc", [128, D // 32], F32)
        wint = _sb(nc, es, "wint", [128, D], I16)

        s_xld = nc.semaphore("s_xld").__enter__()
        s_mo = nc.semaphore("s_mo").__enter__()
        r_xno = Ring(nc, "r_xno", 2)
        r_wld = Ring(nc, "r_wld", 2)
        r_wo = Ring(nc, "r_wo", 2)
        V = Eng(nc, "cV")
        A = Eng(nc, "cA")
        G = Eng(nc, "cG")
        SY = Eng(nc, "cS")
        M = {}

        def body_v(E):
            E.wait(s_xld, 16 * NTILE)
            for i in range(NTILE):
                xti = xt[:, D * i:D * (i + 1)]
                E.do(lambda xti=xti: E.h.bn_stats(out=stats[:, 0:6],
                                                  in_=xti[:, 0:512]))
                E.do(lambda xti=xti: E.h.bn_stats(out=stats[:, 6:12],
                                                  in_=xti[:, 512:1024]))
                E.do(lambda: E.h.bn_aggr(out=pair2[:, 0:2],
                                         in_=stats[:, 0:12]))
                E.do(lambda i=i: E.h.tensor_copy(out=mu[:, i:i + 1],
                                                 in_=pair2[:, 0:1]))
                E.do(lambda i=i: E.h.tensor_copy(out=var[:, i:i + 1],
                                                 in_=pair2[:, 1:2]))
                E.do(lambda xti=xti, i=i: E.h.tensor_scalar(
                    out=dump_v[:], in0=xti, scalar1=1.0, scalar2=None,
                    op0=ALU.mult, op1=ALU.max, accum_out=mxw[:, i:i + 1]))
                E.do(lambda xti=xti, i=i: E.h.tensor_scalar(
                    out=dump_v[:], in0=xti, scalar1=1.0, scalar2=None,
                    op0=ALU.mult, op1=ALU.min, accum_out=mnw[:, i:i + 1]))
            M["veps"] = E.do(lambda: E.h.tensor_scalar(
                out=scr1[:, 0:NTILE], in0=var[:, 0:NTILE], scalar1=EPS,
                scalar2=None, op0=ALU.add))
            E.weng(A, M.get("sqrt", 0))
            E.do(lambda: E.h.reciprocal(out=rstd[:, 0:NTILE],
                                        in_=rstd[:, 0:NTILE]))
            E.do(lambda: E.h.tensor_tensor(out=nbias[:, 0:NTILE],
                                           in0=mu[:, 0:NTILE],
                                           in1=rstd[:, 0:NTILE],
                                           op=ALU.mult))
            M["affine"] = E.do(lambda: E.h.tensor_scalar(
                out=nbias[:, 0:NTILE], in0=nbias[:, 0:NTILE], scalar1=-1.0,
                scalar2=None, op0=ALU.mult))
            E.do(lambda: E.h.tensor_tensor(out=atok[:, 0:NTILE],
                                           in0=mxw[:, 0:NTILE],
                                           in1=mu[:, 0:NTILE],
                                           op=ALU.subtract))
            E.do(lambda: E.h.tensor_tensor(out=scr1[:, 0:NTILE],
                                           in0=mu[:, 0:NTILE],
                                           in1=mnw[:, 0:NTILE],
                                           op=ALU.subtract))
            E.do(lambda: E.h.tensor_tensor(out=atok[:, 0:NTILE],
                                           in0=atok[:, 0:NTILE],
                                           in1=scr1[:, 0:NTILE], op=ALU.max))
            E.do(lambda: E.h.tensor_tensor(out=atok[:, 0:NTILE],
                                           in0=atok[:, 0:NTILE],
                                           in1=rstd[:, 0:NTILE], op=ALU.mult))
            M["colmax"] = E.do(lambda: E.h.tensor_reduce(
                out=col_scr[:, 0:1], in_=atok[:, 0:NTILE], axis=AX.X,
                op=ALU.max))
            for k in range(WT):
                b = k % 2
                E.rwait(r_wld, k)
                if k >= 2:
                    E.rwait(r_wo, k - 2)
                E.do(lambda b=b: E.h.tensor_reduce(
                    out=wsc[:, :], in_=wt_[b][:].rearrange(
                        "p (g c) -> p g c", c=32), axis=AX.X, op=ALU.max,
                    apply_absolute_value=True))
                E.do(lambda: E.h.tensor_scalar(
                    out=wsc[:, :], in0=wsc[:, :], scalar1=1.0 / 7.0,
                    scalar2=SCALE_FLOOR, op0=ALU.mult, op1=ALU.max))
                E.do(lambda: E.h.reciprocal(out=wrc[:, :], in_=wsc[:, :]))
                E.do(lambda b=b: E.h.tensor_tensor(
                    out=wint[:].rearrange("p (g c) -> p g c", c=32),
                    in0=wt_[b][:].rearrange("p (g c) -> p g c", c=32),
                    in1=wrc[:, :].unsqueeze(2).broadcast_to(
                        [128, D // 32, 32]), op=ALU.mult))
                M[("wdq", k)] = E.do(lambda b=b: E.h.tensor_tensor(
                    out=wst[b][:].rearrange("p (g c) -> p g c", c=32),
                    in0=wint[:].rearrange("p (g c) -> p g c", c=32),
                    in1=wsc[:, :].unsqueeze(2).broadcast_to(
                        [128, D // 32, 32]), op=ALU.mult))

        def body_a(E):
            E.weng(V, M.get("veps", 0))
            M["sqrt"] = E.do(lambda: E.h.activation(
                out=rstd[:, 0:NTILE], in_=scr1[:, 0:NTILE], func=AF.Sqrt))
            E.weng(V, M.get("affine", 0))
            for i in range(NTILE):
                if i >= 2:
                    E.rwait(r_xno, i - 2)
                M[("xn", i)] = E.do(lambda i=i: E.h.activation(
                    out=xnst[i % 2][:], in_=xt[:, D * i:D * (i + 1)],
                    func=AF.Identity, scale=rstd[:, i:i + 1],
                    bias=nbias[:, i:i + 1]))

        def body_g(E):
            for k in range(WT + 1):
                if k < WT:
                    if k >= 2:
                        E.weng(V, M.get(("wdq", k - 2), 0))
                    srcw = wup_in if k < IC else wg_in
                    r = (k % IC) * 128
                    E.dma(lambda srcw=srcw, r=r, k=k: E.h.dma_start(
                        out=wt_[k % 2][:], in_=srcw[r:r + 128, :]),
                        ring=r_wld, item=k)
                if k >= 1:
                    E.weng(V, M.get(("wdq", k - 1), 0))
                    dstw = wupq_out if (k - 1) < IC else wgq_out
                    r = ((k - 1) % IC) * 128
                    E.dma(lambda dstw=dstw, r=r, k=k: E.h.dma_start(
                        out=dstw[r:r + 128, :], in_=wst[(k - 1) % 2][:]),
                        ring=r_wo, item=k - 1)
            E.weng(V, M.get("colmax", 0))
            E.do(lambda: E.h.tensor_reduce(out=m11[0:1, 0:1],
                                           in_=col_scr[:, 0:1], axis=AX.C,
                                           op=ALU.max))
            E.dma(lambda: E.h.dma_start(out=mx_out, in_=m11[:]), sem=s_mo)

        def body_sync(E):
            for i in range(NTILE):
                E.dma(lambda i=i: E.h.dma_start(
                    out=xt[:, D * i:D * (i + 1)],
                    in_=x_in[128 * i:128 * (i + 1), :]), sem=s_xld)
            for i in range(NTILE):
                E.weng(A, M.get(("xn", i), 0))
                E.dma(lambda i=i: E.h.dma_start(
                    out=xn_out[128 * i:128 * (i + 1), :],
                    in_=xnst[i % 2][:]), ring=r_xno, item=i)
            E.rwait_all(r_xno, NTILE)
            E.rwait_all(r_wo, WT)
            E.wait(s_mo, 16)

        _two_pass(nc, [("vector", V, body_v), ("scalar", A, body_a),
                       ("gpsimd", G, body_g), ("sync", SY, body_sync)])
    return nc



def _toeplitz_host(diff_kernel, ma_r_kernel):
    from numpy.lib.stride_tricks import as_strided

    def build(k, ndiag):
        n, L = k.shape
        pad = np.zeros((n, 127 + 128 * (ndiag - 1) + 129), np.float32)
        pad[:, 127:127 + L] = np.asarray(k, np.float32)
        st = pad.strides
        base = pad[:, 127:]
        v = as_strided(base, shape=(n, ndiag, 128, 128),
                       strides=(st[0], 128 * st[1], -st[1], st[1]))
        return np.ascontiguousarray(v)

    return (build(np.asarray(diff_kernel, np.float32), ND_DIFF),
            build(np.asarray(ma_r_kernel, np.float32), ND_MA))


def _filt_src(f):
    g_, j = divmod(f, 8)
    if j < 4:
        return 0, 4 * g_ + j, ND_DIFF
    return 1, 4 * g_ + (j - 4), ND_MA



def build_L2():
    nc = bass.Bass("TRN2", target_bir_lowering=False, debug=False)
    xn_in = nc.dram_tensor("xn16", [TOK, D], F16, kind="ExternalInput").ap()
    wupq_in = nc.dram_tensor("wupq", [I, D], F16, kind="ExternalInput").ap()
    wgq_in = nc.dram_tensor("wgq", [I, D], F16, kind="ExternalInput").ap()
    toe_d_in = nc.dram_tensor("toe_d", [64, ND_DIFF, 128, 128], F32,
                              kind="ExternalInput").ap()
    toe_m_in = nc.dram_tensor("toe_m", [64, ND_MA, 128, 128], F32,
                              kind="ExternalInput").ap()
    wd_in = nc.dram_tensor("wd", [D, I], F32, kind="ExternalInput").ap()
    mx8_in = nc.dram_tensor("mx8", [8, 1], F32, kind="ExternalInput").ap()
    up_out = nc.dram_tensor("up16", [I, TOK], F16, kind="ExternalOutput").ap()
    g_out = nc.dram_tensor("g16", [I, TOK], F16, kind="ExternalOutput").ap()
    s_out = nc.dram_tensor("s16", [TOK, D], F16, kind="ExternalOutput").ap()
    wdq_out = nc.dram_tensor("wdq", [D, I], F16, kind="ExternalOutput").ap()
    mupg_out = nc.dram_tensor("mupg", [1, 2], F32, kind="ExternalOutput").ap()
    sx_scr = nc.dram_tensor("sx_scr", [1, 2], F32)

    NXF = HC * (TOK // 512)
    NEV = IC * 16
    WDT = D // 128

    es = ExitStack()
    with es:
        xn_all = _sb(nc, es, "xn_all", [128, NTILE * D], F16)
        xq = [_sb(nc, es, f"xq{h}", [128, TOK], F16) for h in range(HC)]
        xfm = [_sb(nc, es, f"xfm{i}", [128, 512], F16) for i in range(4)]
        xqi = [_sb(nc, es, f"xqi{i}", [128, 512], I16) for i in range(2)]
        toe = [_sb(nc, es, f"toe{i}", [128, ND_MA * 128], F16)
               for i in range(2)]
        wtl = [_sb(nc, es, f"wtl{i}", [128, 2 * HC * 128], F16)
               for i in range(2)]
        pss = [es.enter_context(nc.psum_tensor(f"ps{i}", [128, 512], F32))
               for i in range(8)]
        evst = [_sb(nc, es, f"evst{i}", [128, 512], F16) for i in range(8)]
        ev_scr = _sb(nc, es, "ev_scr", [128, 512], F16)
        mx_up = _sb(nc, es, "mx_up", [128, 1], F32)
        mn_up = _sb(nc, es, "mn_up", [128, 1], F32)
        mx_g = _sb(nc, es, "mx_g", [128, 1], F32)
        mn_g = _sb(nc, es, "mn_g", [128, 1], F32)
        acc1 = _sb(nc, es, "acc1", [128, 1], F32)
        acc2 = _sb(nc, es, "acc2", [128, 1], F32)
        col2 = _sb(nc, es, "col2", [128, 2], F32)
        m11 = _sb(nc, es, "m11", [1, 2], F32)
        mrow = _sb(nc, es, "mrow", [1, 8], F32)
        sxt = _sb(nc, es, "sxt", [1, 2], F32)
        isx_bc = _sb(nc, es, "isx_bc", [128, 1], F32)
        sx_bc = _sb(nc, es, "sx_bc", [128, 1], F32)
        wdt = [_sb(nc, es, f"wdt{i}", [128, I], F32) for i in range(2)]
        wdst = [_sb(nc, es, f"wdst{i}", [128, I], F16) for i in range(2)]
        wdsc = _sb(nc, es, "wdsc", [128, I // 32], F32)
        wdrc = _sb(nc, es, "wdrc", [128, I // 32], F32)
        wdint = _sb(nc, es, "wdint", [128, I], I16)

        s_mx8 = nc.semaphore("s_mx8").__enter__()
        s_sxw = nc.semaphore("s_sxw").__enter__()
        s_bc = nc.semaphore("s_bc").__enter__()
        s_xnl = nc.semaphore("s_xnl").__enter__()
        s_so = nc.semaphore("s_so").__enter__()
        s_fin = nc.semaphore("s_fin").__enter__()
        r_xfm = Ring(nc, "r_xfm", 4)
        r_toe = Ring(nc, "r_toe", 2)
        r_wtl = Ring(nc, "r_wtl", 2)
        r_sp = Ring(nc, "r_sp", 8)
        r_wdl = Ring(nc, "r_wdl", 2)
        r_wdo = Ring(nc, "r_wdo", 2)
        V = Eng(nc, "cV")
        A = Eng(nc, "cA")
        G = Eng(nc, "cG")
        Tn = Eng(nc, "cT")
        SY = Eng(nc, "cS")
        M = {}

        def body_v(E):
            E.wait(s_mx8, 16)
            E.do(lambda: E.h.tensor_reduce(out=m11[0:1, 0:1],
                                           in_=mrow[0:1, 0:8], axis=AX.X,
                                           op=ALU.max))
            E.do(lambda: E.h.tensor_scalar(out=sxt[0:1, 0:1],
                                           in0=m11[0:1, 0:1],
                                           scalar1=1.0 / QMAX,
                                           scalar2=SCALE_FLOOR,
                                           op0=ALU.mult, op1=ALU.max))
            M["sxt"] = E.do(lambda: E.h.reciprocal(out=sxt[0:1, 1:2],
                                                   in_=sxt[0:1, 0:1]))
            E.wait(s_bc, 32)
            for tile, val in ((mx_up, NEG_BIG), (mx_g, NEG_BIG),
                              (mn_up, POS_BIG), (mn_g, POS_BIG)):
                E.do(lambda tile=tile, val=val: E.h.memset(tile[:], val))
            for k in range(NXF):
                h_, tc = divmod(k, TOK // 512)
                E.rwait(r_xfm, k)
                E.do(lambda k=k: E.h.tensor_scalar(
                    out=xqi[k % 2][:], in0=xfm[k % 4][:],
                    scalar1=isx_bc[:, 0:1], scalar2=None, op0=ALU.mult))
                M[("xq", k)] = E.do(lambda h_=h_, tc=tc, k=k: E.h.tensor_copy(
                    out=xq[h_][:, 512 * tc:512 * (tc + 1)],
                    in_=xqi[k % 2][:]))
            for k in range(WDT):
                b = k % 2
                E.rwait(r_wdl, k)
                if k >= 2:
                    E.rwait(r_wdo, k - 2)
                E.do(lambda b=b: E.h.tensor_reduce(
                    out=wdsc[:, :], in_=wdt[b][:].rearrange(
                        "p (g c) -> p g c", c=32), axis=AX.X, op=ALU.max,
                    apply_absolute_value=True))
                E.do(lambda: E.h.tensor_scalar(
                    out=wdsc[:, :], in0=wdsc[:, :], scalar1=1.0 / 7.0,
                    scalar2=SCALE_FLOOR, op0=ALU.mult, op1=ALU.max))
                E.do(lambda: E.h.reciprocal(out=wdrc[:, :], in_=wdsc[:, :]))
                E.do(lambda b=b: E.h.tensor_tensor(
                    out=wdint[:].rearrange("p (g c) -> p g c", c=32),
                    in0=wdt[b][:].rearrange("p (g c) -> p g c", c=32),
                    in1=wdrc[:, :].unsqueeze(2).broadcast_to(
                        [128, I // 32, 32]), op=ALU.mult))
                M[("wdq", k)] = E.do(lambda b=b: E.h.tensor_tensor(
                    out=wdst[b][:].rearrange("p (g c) -> p g c", c=32),
                    in0=wdint[:].rearrange("p (g c) -> p g c", c=32),
                    in1=wdsc[:, :].unsqueeze(2).broadcast_to(
                        [128, I // 32, 32]), op=ALU.mult))
            for k in range(NEV):
                iw = (k // 8) % 2
                E.weng(A, M.get(("ev", k), 0))
                st = evst[k % 8]
                mxt = mx_up if iw == 0 else mx_g
                mnt = mn_up if iw == 0 else mn_g
                E.do(lambda st=st: E.h.tensor_scalar(
                    out=ev_scr[:], in0=st[:], scalar1=1.0, scalar2=None,
                    op0=ALU.mult, op1=ALU.max, accum_out=acc1[:, 0:1]))
                E.do(lambda mxt=mxt: E.h.tensor_tensor(
                    out=mxt[:], in0=mxt[:], in1=acc1[:, 0:1], op=ALU.max))
                E.do(lambda st=st: E.h.tensor_scalar(
                    out=ev_scr[:], in0=st[:], scalar1=1.0, scalar2=None,
                    op0=ALU.mult, op1=ALU.min, accum_out=acc2[:, 0:1]))
                M[("mxp", k)] = E.do(lambda mnt=mnt: E.h.tensor_tensor(
                    out=mnt[:], in0=mnt[:], in1=acc2[:, 0:1], op=ALU.min))
            _absmax_col(E, mx_up, mn_up, col2[:, 0:1])
            M["cols"] = _absmax_col(E, mx_g, mn_g, col2[:, 1:2])

        def body_a(E):
            for f in range(NF):
                for b in range(BPC):
                    E.weng(Tn, M.get(("cmm", f, b), 0))
                    bank = pss[4 + 2 * (f % 2) + b]
                    M[("cev", f, b)] = E.do(
                        lambda f=f, b=b, bank=bank: E.h.activation(
                            out=xn_all[:].rearrange(
                                "p (bb tb dd) -> p bb tb dd", bb=BPC,
                                tb=TB)[:, b, 0:TB, 8 * f:8 * f + 8],
                            in_=bank[:, 0:128].rearrange(
                                "p (a c) -> p a c", c=8),
                            func=AF.Identity))
            M["cev_all"] = M[("cev", NF - 1, BPC - 1)]
            for k in range(NEV):
                Icl = k // 16
                iw = (k // 8) % 2
                half = (k // 4) % 2
                T4 = k % 4
                E.weng(Tn, M.get(("mm", k), 0))
                if k >= 8:
                    E.weng(V, M.get(("mxp", k - 8), 0))
                    E.rwait(r_sp, k - 8)
                bank = pss[4 * iw + T4]
                st = evst[k % 8]
                M[("ev", k)] = E.do(lambda st=st, bank=bank: E.h.activation(
                    out=st[:], in_=bank[:], func=AF.Identity,
                    scale=sx_bc[:, 0:1]))
                tc = 2 * T4 + half
                dst = up_out if iw == 0 else g_out
                E.dma(lambda dst=dst, Icl=Icl, tc=tc, st=st: E.h.dma_start(
                    out=dst[128 * Icl:128 * (Icl + 1),
                            512 * tc:512 * (tc + 1)], in_=st[:]),
                    ring=r_sp, item=k)

        def body_t(E):
            E.wait(s_xnl, 16 * NTILE)
            for f in range(NF):
                _, _, nd = _filt_src(f)
                E.rwait(r_toe, f)
                for b in range(BPC):
                    bank = pss[4 + 2 * (f % 2) + b]
                    if f >= 2:
                        E.weng(A, M.get(("cev", f - 2, b), 0))
                    for d in range(nd):
                        nt = TB - d

                        def mk(d=d, nt=nt, f=f, b=b, bank=bank, nd=nd):
                            return E.h.matmul(
                                bank[:, 8 * d:8 * d + 8 * nt].rearrange(
                                    "p (a c) -> p a c", c=8),
                                toe[f % 2][:, 128 * d:128 * (d + 1)],
                                xn_all[:].rearrange(
                                    "p (bb tb dd) -> p bb tb dd",
                                    bb=BPC, tb=TB)[:, b, 0:nt,
                                                   8 * f:8 * f + 8],
                                start=(d == 0), stop=(d == nd - 1))
                        if d < nd - 1:
                            E.raw(mk)
                        else:
                            M[("cmm", f, b)] = E.do(mk, chain=False)
            kg = 0
            for Icl in range(IC):
                E.rwait(r_wtl, Icl, per=256)
                for iw in range(2):
                    for half in range(2):
                        for T4 in range(4):
                            bank = pss[4 * iw + T4]
                            if Icl == 0 and iw == 1 and half == 0:
                                E.weng(A, M.get("cev_all", 0))
                            if half == 1:
                                prev = 16 * Icl + 8 * iw + T4
                            elif Icl >= 1:
                                prev = 16 * (Icl - 1) + 8 * iw + 4 + T4
                            else:
                                prev = None
                            if prev is not None:
                                E.weng(A, M.get(("ev", prev), 0))
                            if kg == 0:
                                E.weng(V, M.get(("xq", NXF - 1), 0))
                            tc = 2 * T4 + half
                            for h_ in range(HC):
                                def mk(h_=h_, Icl=Icl, iw=iw, tc=tc,
                                       bank=bank):
                                    return E.h.matmul(
                                        bank[:],
                                        wtl[Icl % 2][
                                            :, 128 * (HC * iw + h_):
                                            128 * (HC * iw + h_) + 128],
                                        xq[h_][:, 512 * tc:512 * (tc + 1)],
                                        start=(h_ == 0),
                                        stop=(h_ == HC - 1))
                                if h_ < HC - 1:
                                    E.raw(mk)
                                else:
                                    M[("mm", kg)] = E.do(mk, chain=False)
                            kg += 1

        def body_g(E):
            E.weng(V, M.get("sxt", 0))
            E.dma(lambda: E.h.dma_start(out=sx_scr.ap(), in_=sxt[0:1, 0:2]),
                  sem=s_sxw)
            for f in range(NF):
                kind, row, nd = _filt_src(f)
                srct = toe_d_in if kind == 0 else toe_m_in
                if f >= 2:
                    E.weng(Tn, M.get(("cmm", f - 2, BPC - 1), 0))
                E.dma(lambda srct=srct, row=row, nd=nd, f=f: E.h.dma_start(
                    out=toe[f % 2][:, 0:128 * nd].rearrange(
                        "p (d t) -> p d t", t=128),
                    in_=srct[row, 0:nd, :, :].rearrange("d s t -> s d t")),
                    ring=r_toe, item=f)
            E.weng(A, M.get("cev_all", 0))
            for i in range(NTILE):
                E.dma(lambda i=i: E.h.dma_start(
                    out=s_out[128 * i:128 * (i + 1), :],
                    in_=xn_all[:, D * i:D * (i + 1)]), sem=s_so)
            for k in range(WDT):
                E.weng(V, M.get(("wdq", k), 0))
                E.dma(lambda k=k: E.h.dma_start(
                    out=wdq_out[128 * k:128 * (k + 1), :],
                    in_=wdst[k % 2][:]), ring=r_wdo, item=k)
            E.weng(V, M.get("cols", 0))
            E.do(lambda: E.h.tensor_reduce(out=m11[0:1, 0:2],
                                           in_=col2[:, 0:2], axis=AX.C,
                                           op=ALU.max))
            E.dma(lambda: E.h.dma_start(out=mupg_out, in_=m11[0:1, 0:2]),
                  sem=s_fin)

        def body_sync(E):
            E.dma(lambda: E.h.dma_start(
                out=mrow[0:1, 0:8], in_=mx8_in.rearrange("a b -> b a")),
                sem=s_mx8)
            for i in range(NTILE):
                E.dma(lambda i=i: E.h.dma_start(
                    out=xn_all[:, D * i:D * (i + 1)],
                    in_=xn_in[128 * i:128 * (i + 1), :]), sem=s_xnl)
            E.wait(s_sxw, 16)
            E.dma(lambda: E.h.dma_start(
                out=sx_bc[:], in_=sx_scr.ap()[0:1, 0:1]
                .partition_broadcast(128).squeeze(1)), sem=s_bc)
            E.dma(lambda: E.h.dma_start(
                out=isx_bc[:], in_=sx_scr.ap()[0:1, 1:2]
                .partition_broadcast(128).squeeze(1)), sem=s_bc)
            for k in range(2):
                E.dma(lambda k=k: E.h.dma_start(
                    out=wdt[k % 2][:],
                    in_=wd_in[128 * k:128 * (k + 1), :]), ring=r_wdl, item=k)
            for k in range(NXF):
                h_, tc = divmod(k, TOK // 512)
                if k >= 4:
                    E.weng(V, M.get(("xq", k - 4), 0))
                E.dma(lambda h_=h_, tc=tc, k=k: E.h.dma_start_transpose(
                    xfm[k % 4][:],
                    xn_in[512 * tc:512 * (tc + 1),
                          128 * h_:128 * (h_ + 1)]), ring=r_xfm, item=k)
            for k in range(2, WDT):
                E.weng(V, M.get(("wdq", k - 2), 0))
                E.dma(lambda k=k: E.h.dma_start(
                    out=wdt[k % 2][:],
                    in_=wd_in[128 * k:128 * (k + 1), :]), ring=r_wdl, item=k)
            for Icl in range(IC):
                if Icl >= 2:
                    E.weng(Tn, M.get(("mm", 16 * (Icl - 1) - 1), 0))
                for iw in range(2):
                    wsrc = wupq_in if iw == 0 else wgq_in
                    for h_ in range(HC):
                        E.dma(lambda wsrc=wsrc, Icl=Icl, iw=iw, h_=h_:
                              E.h.dma_start_transpose(
                                  wtl[Icl % 2][
                                      :, 128 * (HC * iw + h_):
                                      128 * (HC * iw + h_) + 128],
                                  wsrc[128 * Icl:128 * (Icl + 1),
                                       128 * h_:128 * (h_ + 1)]),
                              ring=r_wtl, item=Icl)
            E.wait(s_so, 16 * NTILE)
            E.rwait_all(r_sp, NEV)
            E.rwait_all(r_wdo, WDT)
            E.wait(s_fin, 16)

        _two_pass(nc, [("vector", V, body_v), ("scalar", A, body_a),
                       ("tensor", Tn, body_t), ("gpsimd", G, body_g),
                       ("sync", SY, body_sync)])
    return nc



def build_L3():
    nc = bass.Bass("TRN2", target_bir_lowering=False, debug=False)
    g_in = nc.dram_tensor("g16", [I, TOK], F16, kind="ExternalInput").ap()
    mupg8_in = nc.dram_tensor("mupg8", [8, 2], F32, kind="ExternalInput").ap()
    t_out = nc.dram_tensor("t16", [I, TOK], F16, kind="ExternalOutput").ap()
    mt_out = nc.dram_tensor("mt1", [1, 1], F32, kind="ExternalOutput").ap()
    sg_scr = nc.dram_tensor("sg_scr", [1, 3], F32)

    es = ExitStack()
    with es:
        gt = [_sb(nc, es, f"gt{i}", [128, CW], F16) for i in range(3)]
        rg = [_sb(nc, es, f"rg{i}", [128, CW], I16) for i in range(2)]
        sgf = [_sb(nc, es, f"sgf{i}", [128, CW], F32) for i in range(2)]
        u16 = [_sb(nc, es, f"u16_{i}", [128, CW], U16) for i in range(2)]
        tt = [_sb(nc, es, f"tt{i}", [128, CW], F16) for i in range(3)]
        scrm = _sb(nc, es, "scrm", [128, CW], F16)
        mx_t = _sb(nc, es, "mx_t", [128, 1], F32)
        mn_t = _sb(nc, es, "mn_t", [128, 1], F32)
        acc1 = _sb(nc, es, "acc1", [128, 1], F32)
        acc2 = _sb(nc, es, "acc2", [128, 1], F32)
        col_scr = _sb(nc, es, "col_scr", [128, 1], F32)
        m11 = _sb(nc, es, "m11", [1, 1], F32)
        mrow = _sb(nc, es, "mrow", [1, 8], F32)
        sgt = _sb(nc, es, "sgt", [1, 3], F32)
        isg_bc = _sb(nc, es, "isg_bc", [128, 1], F32)
        sg_bc = _sb(nc, es, "sg_bc", [128, 1], F32)
        sg16_bc = _sb(nc, es, "sg16_bc", [128, 1], F32)

        s_mg8 = nc.semaphore("s_mg8").__enter__()
        s_sgw = nc.semaphore("s_sgw").__enter__()
        s_bc = nc.semaphore("s_bc").__enter__()
        s_fin = nc.semaphore("s_fin").__enter__()
        r_ld = Ring(nc, "r_ld", 3)
        r_to = Ring(nc, "r_to", 3)
        V = Eng(nc, "cV")
        A = Eng(nc, "cA")
        G = Eng(nc, "cG")
        SY = Eng(nc, "cS")
        M = {}

        def body_v(E):
            E.wait(s_mg8, 16)
            E.do(lambda: E.h.tensor_reduce(out=m11[0:1, 0:1],
                                           in_=mrow[0:1, 0:8], axis=AX.X,
                                           op=ALU.max))
            E.do(lambda: E.h.tensor_scalar(out=sgt[0:1, 0:1],
                                           in0=m11[0:1, 0:1],
                                           scalar1=1.0 / QMAX,
                                           scalar2=SCALE_FLOOR,
                                           op0=ALU.mult, op1=ALU.max))
            E.do(lambda: E.h.reciprocal(out=sgt[0:1, 1:2], in_=sgt[0:1, 0:1]))
            M["sgt"] = E.do(lambda: E.h.tensor_scalar(
                out=sgt[0:1, 2:3], in0=sgt[0:1, 0:1], scalar1=1.0 / 65536.0,
                scalar2=None, op0=ALU.mult))
            E.wait(s_bc, 48)
            E.do(lambda: E.h.memset(mx_t[:], NEG_BIG))
            E.do(lambda: E.h.memset(mn_t[:], POS_BIG))
            for k in range(NT34 + 1):
                if k < NT34:
                    E.rwait(r_ld, k)
                    if k >= 2:
                        E.weng(A, M.get(("sig", k - 2), 0))
                    M[("rg", k)] = E.do(lambda k=k: E.h.tensor_scalar(
                        out=rg[k % 2][:], in0=gt[k % 3][:],
                        scalar1=isg_bc[:, 0:1], scalar2=None, op0=ALU.mult))
                if k >= 1:
                    j = k - 1
                    E.weng(A, M.get(("sig", j), 0))
                    M[("u16", j)] = E.do(lambda j=j: E.h.tensor_scalar(
                        out=u16[j % 2][:], in0=sgf[j % 2][:],
                        scalar1=65536.0, scalar2=65535.0, op0=ALU.mult,
                        op1=ALU.min))
                    if j >= 3:
                        E.rwait(r_to, j - 3)
                    M[("t", j)] = E.do(lambda j=j: E.h.scalar_tensor_tensor(
                        out=tt[j % 3][:], in0=rg[j % 2][:],
                        scalar=sg16_bc[:, 0:1], in1=u16[j % 2][:],
                        op0=ALU.mult, op1=ALU.mult))
                    E.do(lambda j=j: E.h.tensor_scalar(
                        out=scrm[:], in0=tt[j % 3][:], scalar1=1.0,
                        scalar2=None, op0=ALU.mult, op1=ALU.max,
                        accum_out=acc1[:, 0:1]))
                    E.do(lambda: E.h.tensor_tensor(
                        out=mx_t[:], in0=mx_t[:], in1=acc1[:, 0:1],
                        op=ALU.max))
                    E.do(lambda j=j: E.h.tensor_scalar(
                        out=scrm[:], in0=tt[j % 3][:], scalar1=1.0,
                        scalar2=None, op0=ALU.mult, op1=ALU.min,
                        accum_out=acc2[:, 0:1]))
                    E.do(lambda: E.h.tensor_tensor(
                        out=mn_t[:], in0=mn_t[:], in1=acc2[:, 0:1],
                        op=ALU.min))
            M["col"] = _absmax_col(E, mx_t, mn_t, col_scr[:, 0:1])

        def body_a(E):
            for k in range(NT34):
                E.weng(V, M.get(("rg", k), 0))
                if k >= 2:
                    E.weng(V, M.get(("u16", k - 2), 0))
                M[("sig", k)] = E.do(lambda k=k: E.h.activation(
                    out=sgf[k % 2][:], in_=rg[k % 2][:], func=AF.Sigmoid,
                    scale=sg_bc[:, 0:1]))

        def body_g(E):
            E.weng(V, M.get("sgt", 0))
            E.dma(lambda: E.h.dma_start(out=sg_scr.ap(), in_=sgt[0:1, 0:3]),
                  sem=s_sgw)
            E.weng(V, M.get("col", 0))
            E.do(lambda: E.h.tensor_reduce(out=m11[0:1, 0:1],
                                           in_=col_scr[:, 0:1], axis=AX.C,
                                           op=ALU.max))
            E.dma(lambda: E.h.dma_start(out=mt_out, in_=m11[0:1, 0:1]),
                  sem=s_fin)

        def body_sync(E):
            E.dma(lambda: E.h.dma_start(
                out=mrow[0:1, 0:8],
                in_=mupg8_in[:, 1:2].rearrange("a b -> b a"),
                allow_slow_non_contiguous=True), sem=s_mg8)
            E.wait(s_sgw, 16)
            for i, bc in enumerate((sg_bc, isg_bc, sg16_bc)):
                E.dma(lambda i=i, bc=bc: E.h.dma_start(
                    out=bc[:], in_=sg_scr.ap()[0:1, i:i + 1]
                    .partition_broadcast(128).squeeze(1)), sem=s_bc)
            for k in range(NT34 + 1):
                if k < NT34:
                    Icl, half = divmod(k, 2)
                    if k >= 3:
                        E.weng(V, M.get(("rg", k - 3), 0))
                    E.dma(lambda Icl=Icl, half=half, k=k: E.h.dma_start(
                        out=gt[k % 3][:],
                        in_=g_in[128 * Icl:128 * (Icl + 1),
                                 CW * half:CW * (half + 1)]),
                        ring=r_ld, item=k)
                if k >= 1:
                    j = k - 1
                    Icl, half = divmod(j, 2)
                    E.weng(V, M.get(("t", j), 0))
                    E.dma(lambda Icl=Icl, half=half, j=j: E.h.dma_start(
                        out=t_out[128 * Icl:128 * (Icl + 1),
                                  CW * half:CW * (half + 1)],
                        in_=tt[j % 3][:]), ring=r_to, item=j)
            E.rwait_all(r_to, NT34)
            E.wait(s_fin, 16)

        _two_pass(nc, [("vector", V, body_v), ("scalar", A, body_a),
                       ("gpsimd", G, body_g), ("sync", SY, body_sync)])
    return nc



def build_L4():
    nc = bass.Bass("TRN2", target_bir_lowering=False, debug=False)
    t_in = nc.dram_tensor("t16", [I, TOK], F16, kind="ExternalInput").ap()
    up_in = nc.dram_tensor("up16", [I, TOK], F16, kind="ExternalInput").ap()
    mupg8_in = nc.dram_tensor("mupg8", [8, 2], F32, kind="ExternalInput").ap()
    mt8_in = nc.dram_tensor("mt8", [8, 1], F32, kind="ExternalInput").ap()
    z_out = nc.dram_tensor("z16", [I, TOK], F16, kind="ExternalOutput").ap()
    mz_out = nc.dram_tensor("mz1", [1, 1], F32, kind="ExternalOutput").ap()
    sc_scr = nc.dram_tensor("sc_scr", [1, 3], F32)

    es = ExitStack()
    with es:
        tt = [_sb(nc, es, f"tt{i}", [128, CW], F16) for i in range(3)]
        ut = [_sb(nc, es, f"ut{i}", [128, CW], F16) for i in range(3)]
        g2 = [_sb(nc, es, f"g2_{i}", [128, CW], I16) for i in range(2)]
        ui = [_sb(nc, es, f"ui{i}", [128, CW], I16) for i in range(2)]
        zz = [_sb(nc, es, f"zz{i}", [128, CW], F16) for i in range(3)]
        scrm = _sb(nc, es, "scrm", [128, CW], F16)
        mx_z = _sb(nc, es, "mx_z", [128, 1], F32)
        mn_z = _sb(nc, es, "mn_z", [128, 1], F32)
        acc1 = _sb(nc, es, "acc1", [128, 1], F32)
        acc2 = _sb(nc, es, "acc2", [128, 1], F32)
        col_scr = _sb(nc, es, "col_scr", [128, 1], F32)
        m11 = _sb(nc, es, "m11", [1, 2], F32)
        mrow = _sb(nc, es, "mrow", [1, 8], F32)
        mrow2 = _sb(nc, es, "mrow2", [1, 8], F32)
        sct = _sb(nc, es, "sct", [1, 3], F32)
        is2_bc = _sb(nc, es, "is2_bc", [128, 1], F32)
        isu_bc = _sb(nc, es, "isu_bc", [128, 1], F32)
        ssu_bc = _sb(nc, es, "ssu_bc", [128, 1], F32)

        s_m8 = nc.semaphore("s_m8").__enter__()
        s_scw = nc.semaphore("s_scw").__enter__()
        s_bc = nc.semaphore("s_bc").__enter__()
        s_fin = nc.semaphore("s_fin").__enter__()
        r_tld = Ring(nc, "r_tld", 3)
        r_uld = Ring(nc, "r_uld", 3)
        r_zo = Ring(nc, "r_zo", 3)
        V = Eng(nc, "cV")
        A = Eng(nc, "cA")
        G = Eng(nc, "cG")
        SY = Eng(nc, "cS")
        M = {}

        def body_v(E):
            E.wait(s_m8, 32)
            E.do(lambda: E.h.tensor_reduce(out=m11[0:1, 0:1],
                                           in_=mrow[0:1, 0:8], axis=AX.X,
                                           op=ALU.max))
            E.do(lambda: E.h.tensor_reduce(out=m11[0:1, 1:2],
                                           in_=mrow2[0:1, 0:8], axis=AX.X,
                                           op=ALU.max))
            E.do(lambda: E.h.tensor_scalar(out=acc1[0:1, 0:1],
                                           in0=m11[0:1, 1:2],
                                           scalar1=1.0 / QMAX,
                                           scalar2=SCALE_FLOOR,
                                           op0=ALU.mult, op1=ALU.max))
            E.do(lambda: E.h.tensor_scalar(out=acc2[0:1, 0:1],
                                           in0=m11[0:1, 0:1],
                                           scalar1=1.0 / QMAX,
                                           scalar2=SCALE_FLOOR,
                                           op0=ALU.mult, op1=ALU.max))
            E.do(lambda: E.h.reciprocal(out=sct[0:1, 0:1],
                                        in_=acc1[0:1, 0:1]))
            E.do(lambda: E.h.reciprocal(out=sct[0:1, 1:2],
                                        in_=acc2[0:1, 0:1]))
            M["sct"] = E.do(lambda: E.h.tensor_tensor(
                out=sct[0:1, 2:3], in0=acc1[0:1, 0:1], in1=acc2[0:1, 0:1],
                op=ALU.mult))
            E.wait(s_bc, 48)
            E.do(lambda: E.h.memset(mx_z[:], NEG_BIG))
            E.do(lambda: E.h.memset(mn_z[:], POS_BIG))
            for k in range(NT34):
                E.rwait(r_tld, k)
                M[("g2", k)] = E.do(lambda k=k: E.h.tensor_scalar(
                    out=g2[k % 2][:], in0=tt[k % 3][:],
                    scalar1=is2_bc[:, 0:1], scalar2=None, op0=ALU.mult))
                E.rwait(r_uld, k)
                M[("ui", k)] = E.do(lambda k=k: E.h.tensor_scalar(
                    out=ui[k % 2][:], in0=ut[k % 3][:],
                    scalar1=isu_bc[:, 0:1], scalar2=None, op0=ALU.mult))
                if k >= 3:
                    E.rwait(r_zo, k - 3)
                M[("z", k)] = E.do(lambda k=k: E.h.scalar_tensor_tensor(
                    out=zz[k % 3][:], in0=g2[k % 2][:],
                    scalar=ssu_bc[:, 0:1], in1=ui[k % 2][:],
                    op0=ALU.mult, op1=ALU.mult))
                E.do(lambda k=k: E.h.tensor_scalar(
                    out=scrm[:], in0=zz[k % 3][:], scalar1=1.0, scalar2=None,
                    op0=ALU.mult, op1=ALU.max, accum_out=acc1[:, 0:1]))
                E.do(lambda: E.h.tensor_tensor(out=mx_z[:], in0=mx_z[:],
                                               in1=acc1[:, 0:1], op=ALU.max))
                E.do(lambda k=k: E.h.tensor_scalar(
                    out=scrm[:], in0=zz[k % 3][:], scalar1=1.0, scalar2=None,
                    op0=ALU.mult, op1=ALU.min, accum_out=acc2[:, 0:1]))
                E.do(lambda: E.h.tensor_tensor(out=mn_z[:], in0=mn_z[:],
                                               in1=acc2[:, 0:1], op=ALU.min))
            M["col"] = _absmax_col(E, mx_z, mn_z, col_scr[:, 0:1])

        def body_a(E):
            for k in range(NT34):
                Icl, half = divmod(k, 2)
                if k >= 3:
                    E.weng(V, M.get(("ui", k - 3), 0))
                E.dma(lambda Icl=Icl, half=half, k=k: E.h.dma_start(
                    out=ut[k % 3][:],
                    in_=up_in[128 * Icl:128 * (Icl + 1),
                              CW * half:CW * (half + 1)]),
                    ring=r_uld, item=k)

        def body_g(E):
            E.weng(V, M.get("sct", 0))
            E.dma(lambda: E.h.dma_start(out=sc_scr.ap(), in_=sct[0:1, 0:3]),
                  sem=s_scw)
            E.weng(V, M.get("col", 0))
            E.do(lambda: E.h.tensor_reduce(out=m11[0:1, 0:1],
                                           in_=col_scr[:, 0:1], axis=AX.C,
                                           op=ALU.max))
            E.dma(lambda: E.h.dma_start(out=mz_out, in_=m11[0:1, 0:1]),
                  sem=s_fin)

        def body_sync(E):
            E.dma(lambda: E.h.dma_start(
                out=mrow[0:1, 0:8],
                in_=mupg8_in[:, 0:1].rearrange("a b -> b a"),
                allow_slow_non_contiguous=True), sem=s_m8)
            E.dma(lambda: E.h.dma_start(
                out=mrow2[0:1, 0:8], in_=mt8_in.rearrange("a b -> b a")),
                sem=s_m8)
            E.wait(s_scw, 16)
            for i, bc in enumerate((is2_bc, isu_bc, ssu_bc)):
                E.dma(lambda i=i, bc=bc: E.h.dma_start(
                    out=bc[:], in_=sc_scr.ap()[0:1, i:i + 1]
                    .partition_broadcast(128).squeeze(1)), sem=s_bc)
            for k in range(NT34 + 1):
                if k < NT34:
                    Icl, half = divmod(k, 2)
                    if k >= 3:
                        E.weng(V, M.get(("g2", k - 3), 0))
                    E.dma(lambda Icl=Icl, half=half, k=k: E.h.dma_start(
                        out=tt[k % 3][:],
                        in_=t_in[128 * Icl:128 * (Icl + 1),
                                 CW * half:CW * (half + 1)]),
                        ring=r_tld, item=k)
                if k >= 1:
                    j = k - 1
                    Icl, half = divmod(j, 2)
                    E.weng(V, M.get(("z", j), 0))
                    E.dma(lambda Icl=Icl, half=half, j=j: E.h.dma_start(
                        out=z_out[128 * Icl:128 * (Icl + 1),
                                  CW * half:CW * (half + 1)],
                        in_=zz[j % 3][:]), ring=r_zo, item=j)
            E.rwait_all(r_zo, NT34)
            E.wait(s_fin, 16)

        _two_pass(nc, [("vector", V, body_v), ("scalar", A, body_a),
                       ("gpsimd", G, body_g), ("sync", SY, body_sync)])
    return nc



def build_L5():
    nc = bass.Bass("TRN2", target_bir_lowering=False, debug=False)
    z_in = nc.dram_tensor("z16", [I, TOK], F16, kind="ExternalInput").ap()
    xT_in = nc.dram_tensor("xT", [D, TOK], F32, kind="ExternalInput").ap()
    s_in = nc.dram_tensor("s16", [TOK, D], F16, kind="ExternalInput").ap()
    wdq_in = nc.dram_tensor("wdq", [D, I], F16, kind="ExternalInput").ap()
    mz8_in = nc.dram_tensor("mz8", [8, 1], F32, kind="ExternalInput").ap()
    out_out = nc.dram_tensor("outfm", [D, TOK], F32,
                             kind="ExternalOutput").ap()
    so_scr = nc.dram_tensor("so_scr", [1, 2], F32)

    NTC = TOK // 512
    NJ = NTC * HC

    es = ExitStack()
    with es:
        ot = [_sb(nc, es, f"ot{i}", [128, IC * 512], F16) for i in range(2)]
        oi = [_sb(nc, es, f"oi{i}", [128, 512], I16) for i in range(2)]
        zt = [_sb(nc, es, f"zt{i}", [128, 512], F16) for i in range(3)]
        wdl = _sb(nc, es, "wdl", [128, HC * IC * 128], F16)
        gsil = [_sb(nc, es, f"gsil{i}", [128, 512], F32) for i in range(16)]
        xft = [_sb(nc, es, f"xft{i}", [128, 512], F32) for i in range(6)]
        sft = [_sb(nc, es, f"sft{i}", [128, 512], F16) for i in range(6)]
        ost = [_sb(nc, es, f"ost{i}", [128, 512], F32) for i in range(4)]
        pss = [es.enter_context(nc.psum_tensor(f"ps{i}", [128, 512], F32))
               for i in range(8)]
        mrow = _sb(nc, es, "mrow", [1, 8], F32)
        m11 = _sb(nc, es, "m11", [1, 1], F32)
        sot = _sb(nc, es, "sot", [1, 2], F32)
        iso_bc = _sb(nc, es, "iso_bc", [128, 1], F32)
        so_bc = _sb(nc, es, "so_bc", [128, 1], F32)

        s_mz8 = nc.semaphore("s_mz8").__enter__()
        s_sow = nc.semaphore("s_sow").__enter__()
        s_bc = nc.semaphore("s_bc").__enter__()
        s_wdl = nc.semaphore("s_wdl").__enter__()
        r_zld = Ring(nc, "r_zld", 3)
        r_xld = Ring(nc, "r_xld", 6)
        r_sld = Ring(nc, "r_sld", 6)
        r_out = Ring(nc, "r_out", 4)
        V = Eng(nc, "cV")
        A = Eng(nc, "cA")
        G = Eng(nc, "cG")
        Tn = Eng(nc, "cT")
        SY = Eng(nc, "cS")
        M = {}

        def body_v(E):
            E.wait(s_mz8, 16)
            E.do(lambda: E.h.tensor_reduce(out=m11[0:1, 0:1],
                                           in_=mrow[0:1, 0:8], axis=AX.X,
                                           op=ALU.max))
            E.do(lambda: E.h.tensor_scalar(out=sot[0:1, 0:1],
                                           in0=m11[0:1, 0:1],
                                           scalar1=1.0 / QMAX,
                                           scalar2=SCALE_FLOOR,
                                           op0=ALU.mult, op1=ALU.max))
            M["sot"] = E.do(lambda: E.h.reciprocal(out=sot[0:1, 1:2],
                                                   in_=sot[0:1, 0:1]))
            E.wait(s_bc, 32)
            for tc in range(NTC + 1):
                if tc < NTC:
                    for Icl in range(IC):
                        k = tc * IC + Icl
                        E.rwait(r_zld, k)
                        if Icl == 0 and tc >= 2:
                            E.weng(Tn, M.get(("mm", 8 * (tc - 1) - 1), 0))
                        E.do(lambda k=k: E.h.tensor_scalar(
                            out=oi[k % 2][:], in0=zt[k % 3][:],
                            scalar1=iso_bc[:, 0:1], scalar2=None,
                            op0=ALU.mult))
                        M[("oc", k)] = E.do(
                            lambda tc=tc, Icl=Icl, k=k: E.h.tensor_copy(
                                out=ot[tc % 2][:,
                                               512 * Icl:512 * (Icl + 1)],
                                in_=oi[k % 2][:]))
                if tc >= 1:
                    tcj = tc - 1
                    for hc in range(HC):
                        j = tcj * HC + hc
                        E.weng(A, M.get(("sil", j), 0))
                        E.rwait(r_sld, j)
                        E.rwait(r_xld, j)
                        if j >= 4:
                            E.rwait(r_out, j - 4)
                        E.do(lambda j=j: E.h.tensor_tensor(
                            out=ost[j % 4][:], in0=sft[j % 6][:],
                            in1=gsil[j % 16][:], op=ALU.mult))
                        M[("cmb", j)] = E.do(lambda j=j: E.h.tensor_tensor(
                            out=ost[j % 4][:], in0=ost[j % 4][:],
                            in1=xft[j % 6][:], op=ALU.add))

        def body_t(E):
            E.wait(s_wdl, 16 * HC * IC)
            for tc in range(NTC):
                E.weng(V, M.get(("oc", (tc + 1) * IC - 1), 0))
                for hc in range(HC):
                    j = tc * HC + hc
                    bank = pss[j % 8]
                    if j >= 8:
                        E.weng(A, M.get(("sil", j - 8), 0))
                    for Icl in range(IC):
                        def mk(Icl=Icl, hc=hc, tc=tc, bank=bank):
                            return E.h.matmul(
                                bank[:],
                                wdl[:, 128 * (IC * hc + Icl):
                                    128 * (IC * hc + Icl) + 128],
                                ot[tc % 2][:, 512 * Icl:512 * (Icl + 1)],
                                start=(Icl == 0), stop=(Icl == IC - 1))
                        if Icl < IC - 1:
                            E.raw(mk)
                        else:
                            M[("mm", j)] = E.do(mk, chain=False)

        def body_a(E):
            for j in range(NJ):
                tc, hc = divmod(j, HC)
                if j >= 6:
                    E.weng(V, M.get(("cmb", j - 6), 0))
                E.dma(lambda tc=tc, hc=hc, j=j: E.h.dma_start_transpose(
                    sft[j % 6][:],
                    s_in[512 * tc:512 * (tc + 1),
                         128 * hc:128 * (hc + 1)]), ring=r_sld, item=j)
                E.weng(Tn, M.get(("mm", j), 0))
                if j >= 16:
                    E.weng(V, M.get(("cmb", j - 16), 0))
                M[("sil", j)] = E.do(lambda j=j: E.h.activation(
                    out=gsil[j % 16][:], in_=pss[j % 8][:], func=AF.Silu,
                    scale=so_bc[:, 0:1]))

        def body_g(E):
            E.weng(V, M.get("sot", 0))
            E.dma(lambda: E.h.dma_start(out=so_scr.ap(), in_=sot[0:1, 0:2]),
                  sem=s_sow)
            for j in range(NJ + 1):
                if j < NJ:
                    tc, hc = divmod(j, HC)
                    if j >= 6:
                        E.weng(V, M.get(("cmb", j - 6), 0))
                    E.dma(lambda tc=tc, hc=hc, j=j: E.h.dma_start(
                        out=xft[j % 6][:],
                        in_=xT_in[128 * hc:128 * (hc + 1),
                                  512 * tc:512 * (tc + 1)]),
                        ring=r_xld, item=j)
                if j >= 1:
                    jj = j - 1
                    tc, hc = divmod(jj, HC)
                    E.weng(V, M.get(("cmb", jj), 0))
                    E.dma(lambda tc=tc, hc=hc, jj=jj: E.h.dma_start(
                        out=out_out[128 * hc:128 * (hc + 1),
                                    512 * tc:512 * (tc + 1)],
                        in_=ost[jj % 4][:]), ring=r_out, item=jj)

        def body_sync(E):
            E.dma(lambda: E.h.dma_start(
                out=mrow[0:1, 0:8], in_=mz8_in.rearrange("a b -> b a")),
                sem=s_mz8)
            E.wait(s_sow, 16)
            for i, bc in enumerate((so_bc, iso_bc)):
                E.dma(lambda i=i, bc=bc: E.h.dma_start(
                    out=bc[:], in_=so_scr.ap()[0:1, i:i + 1]
                    .partition_broadcast(128).squeeze(1)), sem=s_bc)
            for hc in range(HC):
                for Icl in range(IC):
                    E.dma(lambda hc=hc, Icl=Icl: E.h.dma_start_transpose(
                        wdl[:, 128 * (IC * hc + Icl):
                            128 * (IC * hc + Icl) + 128],
                        wdq_in[128 * hc:128 * (hc + 1),
                               128 * Icl:128 * (Icl + 1)]), sem=s_wdl)
            for k in range(NTC * IC):
                tc, Icl = divmod(k, IC)
                if k >= 3:
                    E.weng(V, M.get(("oc", k - 3), 0))
                E.dma(lambda tc=tc, Icl=Icl, k=k: E.h.dma_start(
                    out=zt[k % 3][:],
                    in_=z_in[128 * Icl:128 * (Icl + 1),
                             512 * tc:512 * (tc + 1)]), ring=r_zld, item=k)
            E.rwait_all(r_out, NJ)

        _two_pass(nc, [("vector", V, body_v), ("scalar", A, body_a),
                       ("tensor", Tn, body_t), ("gpsimd", G, body_g),
                       ("sync", SY, body_sync)])
    return nc



_CACHE = {}


def _get(name, fn):
    if name not in _CACHE:
        _CACHE[name] = fn()
    return _CACHE[name]


def kernel(L0_attn_u, L0_attn_diff_kernel, L0_attn_ma_r_kernel,
           L0_ffn_w_gate, L0_ffn_w_up, L0_ffn_w_down):
    cores = list(range(NCORES))
    x = np.ascontiguousarray(np.asarray(L0_attn_u, np.float32))
    wg = np.ascontiguousarray(np.asarray(L0_ffn_w_gate, np.float32))
    wu = np.ascontiguousarray(np.asarray(L0_ffn_w_up, np.float32))
    wd = np.ascontiguousarray(np.asarray(L0_ffn_w_down, np.float32))
    toe_d, toe_m = _toeplitz_host(L0_attn_diff_kernel, L0_attn_ma_r_kernel)

    xs = [np.ascontiguousarray(x[BPC * c:BPC * (c + 1)].reshape(TOK, D))
          for c in range(NCORES)]
    xTs = [np.ascontiguousarray(s.T) for s in xs]

    r1 = run_bass_kernel_spmd(
        _get("L1", build_L1),
        [{"x": xs[c], "wup": wu, "wg": wg} for c in cores], core_ids=cores)
    mx8 = np.ascontiguousarray(
        np.stack([r1.results[c]["mx1"][0] for c in cores]))

    r2 = run_bass_kernel_spmd(
        _get("L2", build_L2),
        [{"xn16": r1.results[c]["xn16"], "wupq": r1.results[c]["wupq"],
          "wgq": r1.results[c]["wgq"], "toe_d": toe_d, "toe_m": toe_m,
          "wd": wd, "mx8": mx8} for c in cores], core_ids=cores)
    mupg8 = np.ascontiguousarray(
        np.stack([r2.results[c]["mupg"][0] for c in cores]))

    r3 = run_bass_kernel_spmd(
        _get("L3", build_L3),
        [{"g16": r2.results[c]["g16"], "mupg8": mupg8} for c in cores],
        core_ids=cores)
    mt8 = np.ascontiguousarray(
        np.stack([r3.results[c]["mt1"][0] for c in cores]))

    r4 = run_bass_kernel_spmd(
        _get("L4", build_L4),
        [{"t16": r3.results[c]["t16"], "up16": r2.results[c]["up16"],
          "mupg8": mupg8, "mt8": mt8} for c in cores], core_ids=cores)
    mz8 = np.ascontiguousarray(
        np.stack([r4.results[c]["mz1"][0] for c in cores]))

    r5 = run_bass_kernel_spmd(
        _get("L5", build_L5),
        [{"z16": r4.results[c]["z16"], "xT": xTs[c],
          "s16": r2.results[c]["s16"], "wdq": r2.results[c]["wdq"],
          "mz8": mz8} for c in cores], core_ids=cores)

    out = np.empty((B, T, D), np.float32)
    for c in cores:
        ofm = np.asarray(r5.results[c]["outfm"])
        out[BPC * c:BPC * (c + 1)] = ofm.reshape(D, BPC, T).transpose(1, 2, 0)
    return out


# version 02 (final)
# speedup vs baseline: 1.0908x; 1.0908x over previous
import numpy as np
from contextlib import ExitStack

import concourse.bass as bass
import concourse.mybir as mybir
from concourse.bass_utils import run_bass_kernel_spmd

F32 = mybir.dt.float32
F16 = mybir.dt.float16
I16 = mybir.dt.int16
U16 = mybir.dt.uint16
AF = mybir.ActivationFunctionType
ALU = mybir.AluOpType
AX = mybir.AxisListType

NCORES = 8
B, T, D, I = 16, 2048, 1024, 3072
BPC = B // NCORES
TOK = BPC * T
NTILE = TOK // 128
QMAX = 32767.0
EPS = 1e-5
SCALE_FLOOR = 1e-12
NEG_BIG = -3.0e38
POS_BIG = 3.0e38

ND_MA = 7
ND_DIFF = 2
NF = 128
TB = T // 128
IC = I // 128
HC = D // 128
CW = 2048
NT34 = IC * (TOK // CW)


def _sb(nc, es, name, shape, dt):
    return es.enter_context(nc.sbuf_tensor(name, list(shape), dt))


class Ring:

    def __init__(self, nc, name, nslots):
        self.sems = [nc.semaphore(f"{name}{i}").__enter__()
                     for i in range(nslots)]
        self.n = nslots

    def inc(self, instr, item, amount=16):
        instr.then_inc(self.sems[item % self.n], amount)

    def wait(self, h, item, per=16):
        h.wait_ge(self.sems[item % self.n], per * (item // self.n + 1))

    def wait_all(self, h, nitems, per=16):
        for s in range(self.n):
            cnt = ((nitems - 1 - s) // self.n + 1) if nitems > s else 0
            if cnt:
                h.wait_ge(self.sems[s], per * cnt)


class Eng:

    def __init__(self, nc, name):
        self.sem = nc.semaphore(name).__enter__()
        self.n = 0
        self.h = None
        self.emitting = False

    def start(self, h=None, emitting=False):
        self.n = 0
        self.h = h
        self.emitting = emitting

    def do(self, emit, chain=True):
        if self.emitting:
            if chain and self.n:
                self.h.wait_ge(self.sem, self.n)
            ins = emit()
            ins.then_inc(self.sem, 1)
        self.n += 1
        return self.n

    def raw(self, emit):
        if self.emitting:
            emit()

    def wait(self, sem, val):
        if self.emitting:
            self.h.wait_ge(sem, val)

    def weng(self, other, count):
        if self.emitting and count > 0:
            self.h.wait_ge(other.sem, count)

    def dma(self, emit, ring=None, item=0, sem=None, amount=16):
        if self.emitting:
            if self.n:
                self.h.wait_ge(self.sem, self.n)
            ins = emit()
            if ring is not None:
                ring.inc(ins, item, amount)
            elif sem is not None:
                ins.then_inc(sem, amount)

    def rwait(self, ring, item, per=16):
        if self.emitting:
            ring.wait(self.h, item, per)

    def rwait_all(self, ring, nitems, per=16):
        if self.emitting:
            ring.wait_all(self.h, nitems, per)


def _two_pass(nc, bodies):
    for _, E, fn in bodies:
        E.start(None, False)
        fn(E)
    with nc.Block() as blk:
        for name, E, fn in bodies:
            deco = getattr(blk, name)

            def make(E=E, fn=fn):
                def _(h):
                    E.start(h, True)
                    fn(E)
                return _
            deco(make())


def _absmax_col(E, mx_t, mn_t, col_ap):
    E.do(lambda: E.h.tensor_scalar(out=col_ap, in0=mn_t[:], scalar1=-1.0,
                                   scalar2=None, op0=ALU.mult))
    return E.do(lambda: E.h.tensor_tensor(out=col_ap, in0=col_ap, in1=mx_t[:],
                                          op=ALU.max))



def build_L1():
    nc = bass.Bass("TRN2", target_bir_lowering=False, debug=False)
    x_in = nc.dram_tensor("x", [TOK, D], F32, kind="ExternalInput").ap()
    wup_in = nc.dram_tensor("wup", [I, D], F32, kind="ExternalInput").ap()
    wg_in = nc.dram_tensor("wg", [I, D], F32, kind="ExternalInput").ap()
    xn_out = nc.dram_tensor("xn16", [TOK, D], F16, kind="ExternalOutput").ap()
    wupq_out = nc.dram_tensor("wupq", [I, D], F16, kind="ExternalOutput").ap()
    wgq_out = nc.dram_tensor("wgq", [I, D], F16, kind="ExternalOutput").ap()
    mx_out = nc.dram_tensor("mx1", [1, 1], F32, kind="ExternalOutput").ap()

    WT = 2 * IC

    es = ExitStack()
    with es:
        xt = _sb(nc, es, "xt", [128, NTILE * D], F32)
        xnst = [_sb(nc, es, f"xnst{i}", [128, D], F16) for i in range(2)]
        dump_v = _sb(nc, es, "dump_v", [128, D], F16)
        stats = _sb(nc, es, "stats", [128, 12], F32)
        pair2 = _sb(nc, es, "pair2", [128, 2], F32)
        mu = _sb(nc, es, "mu", [128, NTILE], F32)
        var = _sb(nc, es, "var", [128, NTILE], F32)
        mxw = _sb(nc, es, "mxw", [128, NTILE], F32)
        mnw = _sb(nc, es, "mnw", [128, NTILE], F32)
        rstd = _sb(nc, es, "rstd", [128, NTILE], F32)
        nbias = _sb(nc, es, "nbias", [128, NTILE], F32)
        atok = _sb(nc, es, "atok", [128, NTILE], F32)
        scr1 = _sb(nc, es, "scr1", [128, NTILE], F32)
        col_scr = _sb(nc, es, "col_scr", [128, 1], F32)
        m11 = _sb(nc, es, "m11", [1, 1], F32)
        wt_ = [_sb(nc, es, f"wt{i}", [128, D], F32) for i in range(2)]
        wst = [_sb(nc, es, f"wst{i}", [128, D], F16) for i in range(2)]
        wsc = _sb(nc, es, "wsc", [128, D // 32], F32)
        wrc = _sb(nc, es, "wrc", [128, D // 32], F32)
        wint = _sb(nc, es, "wint", [128, D], I16)

        s_xld = nc.semaphore("s_xld").__enter__()
        s_mo = nc.semaphore("s_mo").__enter__()
        r_xno = Ring(nc, "r_xno", 2)
        r_wld = Ring(nc, "r_wld", 2)
        r_wo = Ring(nc, "r_wo", 2)
        V = Eng(nc, "cV")
        A = Eng(nc, "cA")
        G = Eng(nc, "cG")
        SY = Eng(nc, "cS")
        M = {}

        def body_v(E):
            E.wait(s_xld, 16 * NTILE)
            for i in range(NTILE):
                xti = xt[:, D * i:D * (i + 1)]
                E.do(lambda xti=xti: E.h.bn_stats(out=stats[:, 0:6],
                                                  in_=xti[:, 0:512]))
                E.do(lambda xti=xti: E.h.bn_stats(out=stats[:, 6:12],
                                                  in_=xti[:, 512:1024]))
                E.do(lambda: E.h.bn_aggr(out=pair2[:, 0:2],
                                         in_=stats[:, 0:12]))
                E.do(lambda i=i: E.h.tensor_copy(out=mu[:, i:i + 1],
                                                 in_=pair2[:, 0:1]))
                E.do(lambda i=i: E.h.tensor_copy(out=var[:, i:i + 1],
                                                 in_=pair2[:, 1:2]))
                E.do(lambda xti=xti, i=i: E.h.tensor_scalar(
                    out=dump_v[:], in0=xti, scalar1=1.0, scalar2=None,
                    op0=ALU.mult, op1=ALU.max, accum_out=mxw[:, i:i + 1]))
                E.do(lambda xti=xti, i=i: E.h.tensor_scalar(
                    out=dump_v[:], in0=xti, scalar1=1.0, scalar2=None,
                    op0=ALU.mult, op1=ALU.min, accum_out=mnw[:, i:i + 1]))
            M["veps"] = E.do(lambda: E.h.tensor_scalar(
                out=scr1[:, 0:NTILE], in0=var[:, 0:NTILE], scalar1=EPS,
                scalar2=None, op0=ALU.add))
            E.weng(A, M.get("sqrt", 0))
            E.do(lambda: E.h.reciprocal(out=rstd[:, 0:NTILE],
                                        in_=rstd[:, 0:NTILE]))
            E.do(lambda: E.h.tensor_tensor(out=nbias[:, 0:NTILE],
                                           in0=mu[:, 0:NTILE],
                                           in1=rstd[:, 0:NTILE],
                                           op=ALU.mult))
            M["affine"] = E.do(lambda: E.h.tensor_scalar(
                out=nbias[:, 0:NTILE], in0=nbias[:, 0:NTILE], scalar1=-1.0,
                scalar2=None, op0=ALU.mult))
            E.do(lambda: E.h.tensor_tensor(out=atok[:, 0:NTILE],
                                           in0=mxw[:, 0:NTILE],
                                           in1=mu[:, 0:NTILE],
                                           op=ALU.subtract))
            E.do(lambda: E.h.tensor_tensor(out=scr1[:, 0:NTILE],
                                           in0=mu[:, 0:NTILE],
                                           in1=mnw[:, 0:NTILE],
                                           op=ALU.subtract))
            E.do(lambda: E.h.tensor_tensor(out=atok[:, 0:NTILE],
                                           in0=atok[:, 0:NTILE],
                                           in1=scr1[:, 0:NTILE], op=ALU.max))
            E.do(lambda: E.h.tensor_tensor(out=atok[:, 0:NTILE],
                                           in0=atok[:, 0:NTILE],
                                           in1=rstd[:, 0:NTILE], op=ALU.mult))
            M["colmax"] = E.do(lambda: E.h.tensor_reduce(
                out=col_scr[:, 0:1], in_=atok[:, 0:NTILE], axis=AX.X,
                op=ALU.max))
            for k in range(WT):
                b = k % 2
                E.rwait(r_wld, k)
                if k >= 2:
                    E.rwait(r_wo, k - 2)
                E.do(lambda b=b: E.h.tensor_reduce(
                    out=wsc[:, :], in_=wt_[b][:].rearrange(
                        "p (g c) -> p g c", c=32), axis=AX.X, op=ALU.max,
                    apply_absolute_value=True))
                E.do(lambda: E.h.tensor_scalar(
                    out=wsc[:, :], in0=wsc[:, :], scalar1=1.0 / 7.0,
                    scalar2=SCALE_FLOOR, op0=ALU.mult, op1=ALU.max))
                E.do(lambda: E.h.reciprocal(out=wrc[:, :], in_=wsc[:, :]))
                E.do(lambda b=b: E.h.tensor_tensor(
                    out=wint[:].rearrange("p (g c) -> p g c", c=32),
                    in0=wt_[b][:].rearrange("p (g c) -> p g c", c=32),
                    in1=wrc[:, :].unsqueeze(2).broadcast_to(
                        [128, D // 32, 32]), op=ALU.mult))
                M[("wdq", k)] = E.do(lambda b=b: E.h.tensor_tensor(
                    out=wst[b][:].rearrange("p (g c) -> p g c", c=32),
                    in0=wint[:].rearrange("p (g c) -> p g c", c=32),
                    in1=wsc[:, :].unsqueeze(2).broadcast_to(
                        [128, D // 32, 32]), op=ALU.mult))

        def body_a(E):
            E.weng(V, M.get("veps", 0))
            M["sqrt"] = E.do(lambda: E.h.activation(
                out=rstd[:, 0:NTILE], in_=scr1[:, 0:NTILE], func=AF.Sqrt))
            E.weng(V, M.get("affine", 0))
            for i in range(NTILE):
                if i >= 2:
                    E.rwait(r_xno, i - 2)
                M[("xn", i)] = E.do(lambda i=i: E.h.activation(
                    out=xnst[i % 2][:], in_=xt[:, D * i:D * (i + 1)],
                    func=AF.Identity, scale=rstd[:, i:i + 1],
                    bias=nbias[:, i:i + 1]))

        def body_g(E):
            for k in range(WT + 1):
                if k < WT:
                    if k >= 2:
                        E.weng(V, M.get(("wdq", k - 2), 0))
                    srcw = wup_in if k < IC else wg_in
                    r = (k % IC) * 128
                    E.dma(lambda srcw=srcw, r=r, k=k: E.h.dma_start(
                        out=wt_[k % 2][:], in_=srcw[r:r + 128, :]),
                        ring=r_wld, item=k)
                if k >= 1:
                    E.weng(V, M.get(("wdq", k - 1), 0))
                    dstw = wupq_out if (k - 1) < IC else wgq_out
                    r = ((k - 1) % IC) * 128
                    E.dma(lambda dstw=dstw, r=r, k=k: E.h.dma_start(
                        out=dstw[r:r + 128, :], in_=wst[(k - 1) % 2][:]),
                        ring=r_wo, item=k - 1)
            E.weng(V, M.get("colmax", 0))
            E.do(lambda: E.h.tensor_reduce(out=m11[0:1, 0:1],
                                           in_=col_scr[:, 0:1], axis=AX.C,
                                           op=ALU.max))
            E.dma(lambda: E.h.dma_start(out=mx_out, in_=m11[:]), sem=s_mo)

        def body_sync(E):
            for i in range(NTILE):
                E.dma(lambda i=i: E.h.dma_start(
                    out=xt[:, D * i:D * (i + 1)],
                    in_=x_in[128 * i:128 * (i + 1), :]), sem=s_xld)
            for i in range(NTILE):
                E.weng(A, M.get(("xn", i), 0))
                E.dma(lambda i=i: E.h.dma_start(
                    out=xn_out[128 * i:128 * (i + 1), :],
                    in_=xnst[i % 2][:]), ring=r_xno, item=i)
            E.rwait_all(r_xno, NTILE)
            E.rwait_all(r_wo, WT)
            E.wait(s_mo, 16)

        _two_pass(nc, [("vector", V, body_v), ("scalar", A, body_a),
                       ("gpsimd", G, body_g), ("sync", SY, body_sync)])
    return nc



def _toeplitz_host(diff_kernel, ma_r_kernel):
    from numpy.lib.stride_tricks import as_strided

    def build(k, ndiag):
        n, L = k.shape
        pad = np.zeros((n, 127 + 128 * (ndiag - 1) + 129), np.float32)
        pad[:, 127:127 + L] = np.asarray(k, np.float32)
        st = pad.strides
        base = pad[:, 127:]
        v = as_strided(base, shape=(n, ndiag, 128, 128),
                       strides=(st[0], 128 * st[1], -st[1], st[1]))
        return np.ascontiguousarray(v)

    return (build(np.asarray(diff_kernel, np.float32), ND_DIFF),
            build(np.asarray(ma_r_kernel, np.float32), ND_MA))


def _filt_src(f):
    g_, j = divmod(f, 8)
    if j < 4:
        return 0, 4 * g_ + j, ND_DIFF
    return 1, 4 * g_ + (j - 4), ND_MA



def build_L2():
    nc = bass.Bass("TRN2", target_bir_lowering=False, debug=False)
    xn_in = nc.dram_tensor("xn16", [TOK, D], F16, kind="ExternalInput").ap()
    wupq_in = nc.dram_tensor("wupq", [I, D], F16, kind="ExternalInput").ap()
    wgq_in = nc.dram_tensor("wgq", [I, D], F16, kind="ExternalInput").ap()
    toe_d_in = nc.dram_tensor("toe_d", [64, ND_DIFF, 128, 128], F32,
                              kind="ExternalInput").ap()
    toe_m_in = nc.dram_tensor("toe_m", [64, ND_MA, 128, 128], F32,
                              kind="ExternalInput").ap()
    wd_in = nc.dram_tensor("wd", [D, I], F32, kind="ExternalInput").ap()
    mx8_in = nc.dram_tensor("mx8", [8, 1], F32, kind="ExternalInput").ap()
    up_out = nc.dram_tensor("up16", [I, TOK], F16, kind="ExternalOutput").ap()
    g_out = nc.dram_tensor("g16", [I, TOK], F16, kind="ExternalOutput").ap()
    s_out = nc.dram_tensor("s16", [TOK, D], F16, kind="ExternalOutput").ap()
    wdq_out = nc.dram_tensor("wdq", [D, I], F16, kind="ExternalOutput").ap()
    mupg_out = nc.dram_tensor("mupg", [1, 2], F32, kind="ExternalOutput").ap()
    sx_scr = nc.dram_tensor("sx_scr", [1, 2], F32)

    NXF = HC * (TOK // 512)
    NEV = IC * 16
    WDT = D // 128

    es = ExitStack()
    with es:
        xn_all = _sb(nc, es, "xn_all", [128, NTILE * D], F16)
        xq = [_sb(nc, es, f"xq{h}", [128, TOK], F16) for h in range(HC)]
        xfm = [_sb(nc, es, f"xfm{i}", [128, 512], F16) for i in range(4)]
        xqi = [_sb(nc, es, f"xqi{i}", [128, 512], I16) for i in range(2)]
        toe = [_sb(nc, es, f"toe{i}", [128, ND_MA * 128], F16)
               for i in range(2)]
        wtl = [_sb(nc, es, f"wtl{i}", [128, 2 * HC * 128], F16)
               for i in range(2)]
        pss = [es.enter_context(nc.psum_tensor(f"ps{i}", [128, 512], F32))
               for i in range(8)]
        evst_all = _sb(nc, es, "evst_all", [128, 8 * 512], F16)
        evst = [evst_all[:, 512 * i:512 * (i + 1)] for i in range(8)]
        ev_scr = _sb(nc, es, "ev_scr", [128, 2048], F16)
        wmx = _sb(nc, es, "wmx", [128, 96], F32)
        wmn = _sb(nc, es, "wmn", [128, 96], F32)
        mx_up = _sb(nc, es, "mx_up", [128, 1], F32)
        mn_up = _sb(nc, es, "mn_up", [128, 1], F32)
        mx_g = _sb(nc, es, "mx_g", [128, 1], F32)
        mn_g = _sb(nc, es, "mn_g", [128, 1], F32)
        acc1 = _sb(nc, es, "acc1", [128, 1], F32)
        acc2 = _sb(nc, es, "acc2", [128, 1], F32)
        col2 = _sb(nc, es, "col2", [128, 2], F32)
        m11 = _sb(nc, es, "m11", [1, 2], F32)
        mrow = _sb(nc, es, "mrow", [1, 8], F32)
        sxt = _sb(nc, es, "sxt", [1, 2], F32)
        isx_bc = _sb(nc, es, "isx_bc", [128, 1], F32)
        sx_bc = _sb(nc, es, "sx_bc", [128, 1], F32)
        wdt = [_sb(nc, es, f"wdt{i}", [128, I], F32) for i in range(2)]
        wdst = [_sb(nc, es, f"wdst{i}", [128, I], F16) for i in range(2)]
        wdsc = _sb(nc, es, "wdsc", [128, I // 32], F32)
        wdrc = _sb(nc, es, "wdrc", [128, I // 32], F32)
        wdint = _sb(nc, es, "wdint", [128, I], I16)

        s_mx8 = nc.semaphore("s_mx8").__enter__()
        s_sxw = nc.semaphore("s_sxw").__enter__()
        s_bc = nc.semaphore("s_bc").__enter__()
        s_xnl = nc.semaphore("s_xnl").__enter__()
        s_so = nc.semaphore("s_so").__enter__()
        s_fin = nc.semaphore("s_fin").__enter__()
        r_xfm = Ring(nc, "r_xfm", 4)
        r_toe = Ring(nc, "r_toe", 2)
        r_wtl = Ring(nc, "r_wtl", 2)
        r_sp = Ring(nc, "r_sp", 8)
        r_wdl = Ring(nc, "r_wdl", 2)
        r_wdo = Ring(nc, "r_wdo", 2)
        V = Eng(nc, "cV")
        A = Eng(nc, "cA")
        G = Eng(nc, "cG")
        Tn = Eng(nc, "cT")
        SY = Eng(nc, "cS")
        M = {}

        def body_v(E):
            E.wait(s_mx8, 16)
            E.do(lambda: E.h.tensor_reduce(out=m11[0:1, 0:1],
                                           in_=mrow[0:1, 0:8], axis=AX.X,
                                           op=ALU.max))
            E.do(lambda: E.h.tensor_scalar(out=sxt[0:1, 0:1],
                                           in0=m11[0:1, 0:1],
                                           scalar1=1.0 / QMAX,
                                           scalar2=SCALE_FLOOR,
                                           op0=ALU.mult, op1=ALU.max))
            M["sxt"] = E.do(lambda: E.h.reciprocal(out=sxt[0:1, 1:2],
                                                   in_=sxt[0:1, 0:1]))
            E.wait(s_bc, 32)
            for tile, val in ((mx_up, NEG_BIG), (mx_g, NEG_BIG),
                              (mn_up, POS_BIG), (mn_g, POS_BIG)):
                E.do(lambda tile=tile, val=val: E.h.memset(tile[:], val))
            for k in range(NXF):
                h_, tc = divmod(k, TOK // 512)
                E.rwait(r_xfm, k)
                E.do(lambda k=k: E.h.tensor_scalar(
                    out=xqi[k % 2][:], in0=xfm[k % 4][:],
                    scalar1=isx_bc[:, 0:1], scalar2=None, op0=ALU.mult))
                M[("xq", k)] = E.do(lambda h_=h_, tc=tc, k=k: E.h.tensor_copy(
                    out=xq[h_][:, 512 * tc:512 * (tc + 1)],
                    in_=xqi[k % 2][:]))
            for k in range(WDT):
                b = k % 2
                E.rwait(r_wdl, k)
                if k >= 2:
                    E.rwait(r_wdo, k - 2)
                E.do(lambda b=b: E.h.tensor_reduce(
                    out=wdsc[:, :], in_=wdt[b][:].rearrange(
                        "p (g c) -> p g c", c=32), axis=AX.X, op=ALU.max,
                    apply_absolute_value=True))
                E.do(lambda: E.h.tensor_scalar(
                    out=wdsc[:, :], in0=wdsc[:, :], scalar1=1.0 / 7.0,
                    scalar2=SCALE_FLOOR, op0=ALU.mult, op1=ALU.max))
                E.do(lambda: E.h.reciprocal(out=wdrc[:, :], in_=wdsc[:, :]))
                E.do(lambda b=b: E.h.tensor_tensor(
                    out=wdint[:].rearrange("p (g c) -> p g c", c=32),
                    in0=wdt[b][:].rearrange("p (g c) -> p g c", c=32),
                    in1=wdrc[:, :].unsqueeze(2).broadcast_to(
                        [128, I // 32, 32]), op=ALU.mult))
                M[("wdq", k)] = E.do(lambda b=b: E.h.tensor_tensor(
                    out=wdst[b][:].rearrange("p (g c) -> p g c", c=32),
                    in0=wdint[:].rearrange("p (g c) -> p g c", c=32),
                    in1=wdsc[:, :].unsqueeze(2).broadcast_to(
                        [128, I // 32, 32]), op=ALU.mult))
            for q in range(NEV // 4):
                E.weng(A, M.get(("ev", 4 * q + 3), 0))
                half = 2048 * (q % 2)
                seg = evst_all[:, half:half + 2048]
                E.do(lambda seg=seg, q=q: E.h.tensor_scalar(
                    out=ev_scr[:], in0=seg, scalar1=1.0, scalar2=None,
                    op0=ALU.mult, op1=ALU.max, accum_out=wmx[:, q:q + 1]))
                M[("mxp", q)] = E.do(lambda seg=seg, q=q: E.h.tensor_scalar(
                    out=ev_scr[:], in0=seg, scalar1=1.0, scalar2=None,
                    op0=ALU.mult, op1=ALU.min, accum_out=wmn[:, q:q + 1]))
            for (wsrc, dst, opx) in ((wmx, mx_up, 0), (wmn, mn_up, 0),
                                     (wmx, mx_g, 1), (wmn, mn_g, 1)):
                E.do(lambda wsrc=wsrc, dst=dst, opx=opx: E.h.tensor_reduce(
                    out=dst[:, 0:1],
                    in_=wsrc[:, :].rearrange("p (g e) -> p g e", e=4)[
                        :, :, 2 * opx:2 * opx + 2],
                    axis=AX.XY,
                    op=(ALU.max if wsrc is wmx else ALU.min)))
            _absmax_col(E, mx_up, mn_up, col2[:, 0:1])
            M["cols"] = _absmax_col(E, mx_g, mn_g, col2[:, 1:2])

        def body_a(E):
            for f in range(NF):
                for b in range(BPC):
                    E.weng(Tn, M.get(("cmm", f, b), 0))
                    bank = pss[4 + 2 * (f % 2) + b]
                    M[("cev", f, b)] = E.do(
                        lambda f=f, b=b, bank=bank: E.h.activation(
                            out=xn_all[:].rearrange(
                                "p (bb tb dd) -> p bb tb dd", bb=BPC,
                                tb=TB)[:, b, 0:TB, 8 * f:8 * f + 8],
                            in_=bank[:, 0:128].rearrange(
                                "p (a c) -> p a c", c=8),
                            func=AF.Identity))
            M["cev_all"] = M[("cev", NF - 1, BPC - 1)]
            for k in range(NEV):
                Icl = k // 16
                iw = (k // 8) % 2
                half = (k // 4) % 2
                T4 = k % 4
                E.weng(Tn, M.get(("mm", k), 0))
                if k >= 8:
                    E.weng(V, M.get(("mxp", (k - 8) // 4), 0))
                    E.rwait(r_sp, k - 8)
                bank = pss[4 * iw + T4]
                st = evst[k % 8]
                M[("ev", k)] = E.do(lambda st=st, bank=bank: E.h.activation(
                    out=st[:], in_=bank[:], func=AF.Identity,
                    scale=sx_bc[:, 0:1]))
                tc = 2 * T4 + half
                dst = up_out if iw == 0 else g_out
                E.dma(lambda dst=dst, Icl=Icl, tc=tc, st=st: E.h.dma_start(
                    out=dst[128 * Icl:128 * (Icl + 1),
                            512 * tc:512 * (tc + 1)], in_=st[:]),
                    ring=r_sp, item=k)

        def body_t(E):
            E.wait(s_xnl, 16 * NTILE)
            for f in range(NF):
                _, _, nd = _filt_src(f)
                E.rwait(r_toe, f)
                for b in range(BPC):
                    bank = pss[4 + 2 * (f % 2) + b]
                    if f >= 2:
                        E.weng(A, M.get(("cev", f - 2, b), 0))
                    for d in range(nd):
                        nt = TB - d

                        def mk(d=d, nt=nt, f=f, b=b, bank=bank, nd=nd):
                            return E.h.matmul(
                                bank[:, 8 * d:8 * d + 8 * nt].rearrange(
                                    "p (a c) -> p a c", c=8),
                                toe[f % 2][:, 128 * d:128 * (d + 1)],
                                xn_all[:].rearrange(
                                    "p (bb tb dd) -> p bb tb dd",
                                    bb=BPC, tb=TB)[:, b, 0:nt,
                                                   8 * f:8 * f + 8],
                                start=(d == 0), stop=(d == nd - 1))
                        if d < nd - 1:
                            E.raw(mk)
                        else:
                            M[("cmm", f, b)] = E.do(mk, chain=False)
            kg = 0
            for Icl in range(IC):
                E.rwait(r_wtl, Icl, per=256)
                for iw in range(2):
                    for half in range(2):
                        for T4 in range(4):
                            bank = pss[4 * iw + T4]
                            if Icl == 0 and iw == 1 and half == 0:
                                E.weng(A, M.get("cev_all", 0))
                            if half == 1:
                                prev = 16 * Icl + 8 * iw + T4
                            elif Icl >= 1:
                                prev = 16 * (Icl - 1) + 8 * iw + 4 + T4
                            else:
                                prev = None
                            if prev is not None:
                                E.weng(A, M.get(("ev", prev), 0))
                            if kg == 0:
                                E.weng(V, M.get(("xq", NXF - 1), 0))
                            tc = 2 * T4 + half
                            for h_ in range(HC):
                                def mk(h_=h_, Icl=Icl, iw=iw, tc=tc,
                                       bank=bank):
                                    return E.h.matmul(
                                        bank[:],
                                        wtl[Icl % 2][
                                            :, 128 * (HC * iw + h_):
                                            128 * (HC * iw + h_) + 128],
                                        xq[h_][:, 512 * tc:512 * (tc + 1)],
                                        start=(h_ == 0),
                                        stop=(h_ == HC - 1))
                                if h_ < HC - 1:
                                    E.raw(mk)
                                else:
                                    M[("mm", kg)] = E.do(mk, chain=False)
                            kg += 1

        def body_g(E):
            E.weng(V, M.get("sxt", 0))
            E.dma(lambda: E.h.dma_start(out=sx_scr.ap(), in_=sxt[0:1, 0:2]),
                  sem=s_sxw)
            for f in range(NF):
                kind, row, nd = _filt_src(f)
                srct = toe_d_in if kind == 0 else toe_m_in
                if f >= 2:
                    E.weng(Tn, M.get(("cmm", f - 2, BPC - 1), 0))
                E.dma(lambda srct=srct, row=row, nd=nd, f=f: E.h.dma_start(
                    out=toe[f % 2][:, 0:128 * nd].rearrange(
                        "p (d t) -> p d t", t=128),
                    in_=srct[row, 0:nd, :, :].rearrange("d s t -> s d t")),
                    ring=r_toe, item=f)
            E.weng(A, M.get("cev_all", 0))
            for i in range(NTILE):
                E.dma(lambda i=i: E.h.dma_start(
                    out=s_out[128 * i:128 * (i + 1), :],
                    in_=xn_all[:, D * i:D * (i + 1)]), sem=s_so)
            for k in range(WDT):
                E.weng(V, M.get(("wdq", k), 0))
                E.dma(lambda k=k: E.h.dma_start(
                    out=wdq_out[128 * k:128 * (k + 1), :],
                    in_=wdst[k % 2][:]), ring=r_wdo, item=k)
            E.weng(V, M.get("cols", 0))
            E.do(lambda: E.h.tensor_reduce(out=m11[0:1, 0:2],
                                           in_=col2[:, 0:2], axis=AX.C,
                                           op=ALU.max))
            E.dma(lambda: E.h.dma_start(out=mupg_out, in_=m11[0:1, 0:2]),
                  sem=s_fin)

        def body_sync(E):
            E.dma(lambda: E.h.dma_start(
                out=mrow[0:1, 0:8], in_=mx8_in.rearrange("a b -> b a")),
                sem=s_mx8)
            for i in range(NTILE):
                E.dma(lambda i=i: E.h.dma_start(
                    out=xn_all[:, D * i:D * (i + 1)],
                    in_=xn_in[128 * i:128 * (i + 1), :]), sem=s_xnl)
            E.wait(s_sxw, 16)
            E.dma(lambda: E.h.dma_start(
                out=sx_bc[:], in_=sx_scr.ap()[0:1, 0:1]
                .partition_broadcast(128).squeeze(1)), sem=s_bc)
            E.dma(lambda: E.h.dma_start(
                out=isx_bc[:], in_=sx_scr.ap()[0:1, 1:2]
                .partition_broadcast(128).squeeze(1)), sem=s_bc)
            for k in range(2):
                E.dma(lambda k=k: E.h.dma_start(
                    out=wdt[k % 2][:],
                    in_=wd_in[128 * k:128 * (k + 1), :]), ring=r_wdl, item=k)
            for k in range(NXF):
                h_, tc = divmod(k, TOK // 512)
                if k >= 4:
                    E.weng(V, M.get(("xq", k - 4), 0))
                E.dma(lambda h_=h_, tc=tc, k=k: E.h.dma_start_transpose(
                    xfm[k % 4][:],
                    xn_in[512 * tc:512 * (tc + 1),
                          128 * h_:128 * (h_ + 1)]), ring=r_xfm, item=k)
            for k in range(2, WDT):
                E.weng(V, M.get(("wdq", k - 2), 0))
                E.dma(lambda k=k: E.h.dma_start(
                    out=wdt[k % 2][:],
                    in_=wd_in[128 * k:128 * (k + 1), :]), ring=r_wdl, item=k)
            for Icl in range(IC):
                if Icl >= 2:
                    E.weng(Tn, M.get(("mm", 16 * (Icl - 1) - 1), 0))
                for iw in range(2):
                    wsrc = wupq_in if iw == 0 else wgq_in
                    for h_ in range(HC):
                        E.dma(lambda wsrc=wsrc, Icl=Icl, iw=iw, h_=h_:
                              E.h.dma_start_transpose(
                                  wtl[Icl % 2][
                                      :, 128 * (HC * iw + h_):
                                      128 * (HC * iw + h_) + 128],
                                  wsrc[128 * Icl:128 * (Icl + 1),
                                       128 * h_:128 * (h_ + 1)]),
                              ring=r_wtl, item=Icl)
            E.wait(s_so, 16 * NTILE)
            E.rwait_all(r_sp, NEV)
            E.rwait_all(r_wdo, WDT)
            E.wait(s_fin, 16)

        _two_pass(nc, [("vector", V, body_v), ("scalar", A, body_a),
                       ("tensor", Tn, body_t), ("gpsimd", G, body_g),
                       ("sync", SY, body_sync)])
    return nc



def build_L3():
    nc = bass.Bass("TRN2", target_bir_lowering=False, debug=False)
    g_in = nc.dram_tensor("g16", [I, TOK], F16, kind="ExternalInput").ap()
    mupg8_in = nc.dram_tensor("mupg8", [8, 2], F32, kind="ExternalInput").ap()
    t_out = nc.dram_tensor("t16", [I, TOK], F16, kind="ExternalOutput").ap()
    mt_out = nc.dram_tensor("mt1", [1, 1], F32, kind="ExternalOutput").ap()
    sg_scr = nc.dram_tensor("sg_scr", [1, 3], F32)

    es = ExitStack()
    with es:
        gt = [_sb(nc, es, f"gt{i}", [128, CW], F16) for i in range(3)]
        rg = [_sb(nc, es, f"rg{i}", [128, CW], I16) for i in range(2)]
        sgf = [_sb(nc, es, f"sgf{i}", [128, CW], F32) for i in range(2)]
        u16 = [_sb(nc, es, f"u16_{i}", [128, CW], U16) for i in range(2)]
        tt = [_sb(nc, es, f"tt{i}", [128, CW], F16) for i in range(3)]
        scrm = _sb(nc, es, "scrm", [128, CW], F16)
        mx_t = _sb(nc, es, "mx_t", [128, 1], F32)
        mn_t = _sb(nc, es, "mn_t", [128, 1], F32)
        wtmx = _sb(nc, es, "wtmx", [128, NT34], F32)
        wtmn = _sb(nc, es, "wtmn", [128, NT34], F32)
        acc1 = _sb(nc, es, "acc1", [128, 1], F32)
        acc2 = _sb(nc, es, "acc2", [128, 1], F32)
        col_scr = _sb(nc, es, "col_scr", [128, 1], F32)
        m11 = _sb(nc, es, "m11", [1, 1], F32)
        mrow = _sb(nc, es, "mrow", [1, 8], F32)
        sgt = _sb(nc, es, "sgt", [1, 3], F32)
        isg_bc = _sb(nc, es, "isg_bc", [128, 1], F32)
        sg_bc = _sb(nc, es, "sg_bc", [128, 1], F32)
        sg16_bc = _sb(nc, es, "sg16_bc", [128, 1], F32)

        s_mg8 = nc.semaphore("s_mg8").__enter__()
        s_sgw = nc.semaphore("s_sgw").__enter__()
        s_bc = nc.semaphore("s_bc").__enter__()
        s_fin = nc.semaphore("s_fin").__enter__()
        r_ld = Ring(nc, "r_ld", 3)
        r_to = Ring(nc, "r_to", 3)
        V = Eng(nc, "cV")
        A = Eng(nc, "cA")
        G = Eng(nc, "cG")
        SY = Eng(nc, "cS")
        M = {}

        def body_v(E):
            E.wait(s_mg8, 16)
            E.do(lambda: E.h.tensor_reduce(out=m11[0:1, 0:1],
                                           in_=mrow[0:1, 0:8], axis=AX.X,
                                           op=ALU.max))
            E.do(lambda: E.h.tensor_scalar(out=sgt[0:1, 0:1],
                                           in0=m11[0:1, 0:1],
                                           scalar1=1.0 / QMAX,
                                           scalar2=SCALE_FLOOR,
                                           op0=ALU.mult, op1=ALU.max))
            E.do(lambda: E.h.reciprocal(out=sgt[0:1, 1:2], in_=sgt[0:1, 0:1]))
            M["sgt"] = E.do(lambda: E.h.tensor_scalar(
                out=sgt[0:1, 2:3], in0=sgt[0:1, 0:1], scalar1=1.0 / 65536.0,
                scalar2=None, op0=ALU.mult))
            E.wait(s_bc, 48)
            E.do(lambda: E.h.memset(mx_t[:], NEG_BIG))
            E.do(lambda: E.h.memset(mn_t[:], POS_BIG))
            for k in range(NT34 + 1):
                if k < NT34:
                    E.rwait(r_ld, k)
                    if k >= 2:
                        E.weng(A, M.get(("sig", k - 2), 0))
                    M[("rg", k)] = E.do(lambda k=k: E.h.tensor_scalar(
                        out=rg[k % 2][:], in0=gt[k % 3][:],
                        scalar1=isg_bc[:, 0:1], scalar2=None, op0=ALU.mult))
                if k >= 1:
                    j = k - 1
                    E.weng(A, M.get(("sig", j), 0))
                    M[("u16", j)] = E.do(lambda j=j: E.h.tensor_scalar(
                        out=u16[j % 2][:], in0=sgf[j % 2][:],
                        scalar1=65536.0, scalar2=65535.0, op0=ALU.mult,
                        op1=ALU.min))
                    if j >= 3:
                        E.rwait(r_to, j - 3)
                    M[("t", j)] = E.do(lambda j=j: E.h.scalar_tensor_tensor(
                        out=tt[j % 3][:], in0=rg[j % 2][:],
                        scalar=sg16_bc[:, 0:1], in1=u16[j % 2][:],
                        op0=ALU.mult, op1=ALU.mult))
                    E.do(lambda j=j: E.h.tensor_scalar(
                        out=scrm[:], in0=tt[j % 3][:], scalar1=1.0,
                        scalar2=None, op0=ALU.mult, op1=ALU.max,
                        accum_out=wtmx[:, j:j + 1]))
                    E.do(lambda j=j: E.h.tensor_scalar(
                        out=scrm[:], in0=tt[j % 3][:], scalar1=1.0,
                        scalar2=None, op0=ALU.mult, op1=ALU.min,
                        accum_out=wtmn[:, j:j + 1]))
            E.do(lambda: E.h.tensor_reduce(out=mx_t[:, 0:1],
                                           in_=wtmx[:, 0:NT34], axis=AX.X,
                                           op=ALU.max))
            E.do(lambda: E.h.tensor_reduce(out=mn_t[:, 0:1],
                                           in_=wtmn[:, 0:NT34], axis=AX.X,
                                           op=ALU.min))
            M["col"] = _absmax_col(E, mx_t, mn_t, col_scr[:, 0:1])

        def body_a(E):
            for k in range(NT34):
                E.weng(V, M.get(("rg", k), 0))
                if k >= 2:
                    E.weng(V, M.get(("u16", k - 2), 0))
                M[("sig", k)] = E.do(lambda k=k: E.h.activation(
                    out=sgf[k % 2][:], in_=rg[k % 2][:], func=AF.Sigmoid,
                    scale=sg_bc[:, 0:1]))

        def body_g(E):
            E.weng(V, M.get("sgt", 0))
            E.dma(lambda: E.h.dma_start(out=sg_scr.ap(), in_=sgt[0:1, 0:3]),
                  sem=s_sgw)
            E.weng(V, M.get("col", 0))
            E.do(lambda: E.h.tensor_reduce(out=m11[0:1, 0:1],
                                           in_=col_scr[:, 0:1], axis=AX.C,
                                           op=ALU.max))
            E.dma(lambda: E.h.dma_start(out=mt_out, in_=m11[0:1, 0:1]),
                  sem=s_fin)

        def body_sync(E):
            E.dma(lambda: E.h.dma_start(
                out=mrow[0:1, 0:8],
                in_=mupg8_in[:, 1:2].rearrange("a b -> b a"),
                allow_slow_non_contiguous=True), sem=s_mg8)
            E.wait(s_sgw, 16)
            for i, bc in enumerate((sg_bc, isg_bc, sg16_bc)):
                E.dma(lambda i=i, bc=bc: E.h.dma_start(
                    out=bc[:], in_=sg_scr.ap()[0:1, i:i + 1]
                    .partition_broadcast(128).squeeze(1)), sem=s_bc)
            for k in range(NT34 + 1):
                if k < NT34:
                    Icl, half = divmod(k, 2)
                    if k >= 3:
                        E.weng(V, M.get(("rg", k - 3), 0))
                    E.dma(lambda Icl=Icl, half=half, k=k: E.h.dma_start(
                        out=gt[k % 3][:],
                        in_=g_in[128 * Icl:128 * (Icl + 1),
                                 CW * half:CW * (half + 1)]),
                        ring=r_ld, item=k)
                if k >= 1:
                    j = k - 1
                    Icl, half = divmod(j, 2)
                    E.weng(V, M.get(("t", j), 0))
                    E.dma(lambda Icl=Icl, half=half, j=j: E.h.dma_start(
                        out=t_out[128 * Icl:128 * (Icl + 1),
                                  CW * half:CW * (half + 1)],
                        in_=tt[j % 3][:]), ring=r_to, item=j)
            E.rwait_all(r_to, NT34)
            E.wait(s_fin, 16)

        _two_pass(nc, [("vector", V, body_v), ("scalar", A, body_a),
                       ("gpsimd", G, body_g), ("sync", SY, body_sync)])
    return nc



def build_L4():
    nc = bass.Bass("TRN2", target_bir_lowering=False, debug=False)
    t_in = nc.dram_tensor("t16", [I, TOK], F16, kind="ExternalInput").ap()
    up_in = nc.dram_tensor("up16", [I, TOK], F16, kind="ExternalInput").ap()
    mupg8_in = nc.dram_tensor("mupg8", [8, 2], F32, kind="ExternalInput").ap()
    mt8_in = nc.dram_tensor("mt8", [8, 1], F32, kind="ExternalInput").ap()
    z_out = nc.dram_tensor("z16", [I, TOK], F16, kind="ExternalOutput").ap()
    mz_out = nc.dram_tensor("mz1", [1, 1], F32, kind="ExternalOutput").ap()
    sc_scr = nc.dram_tensor("sc_scr", [1, 3], F32)

    es = ExitStack()
    with es:
        tt = [_sb(nc, es, f"tt{i}", [128, CW], F16) for i in range(3)]
        ut = [_sb(nc, es, f"ut{i}", [128, CW], F16) for i in range(3)]
        g2 = [_sb(nc, es, f"g2_{i}", [128, CW], I16) for i in range(2)]
        ui = [_sb(nc, es, f"ui{i}", [128, CW], I16) for i in range(2)]
        zz = [_sb(nc, es, f"zz{i}", [128, CW], F16) for i in range(3)]
        scrm = _sb(nc, es, "scrm", [128, CW], F16)
        mx_z = _sb(nc, es, "mx_z", [128, 1], F32)
        mn_z = _sb(nc, es, "mn_z", [128, 1], F32)
        wzmx = _sb(nc, es, "wzmx", [128, NT34], F32)
        wzmn = _sb(nc, es, "wzmn", [128, NT34], F32)
        acc1 = _sb(nc, es, "acc1", [128, 1], F32)
        acc2 = _sb(nc, es, "acc2", [128, 1], F32)
        col_scr = _sb(nc, es, "col_scr", [128, 1], F32)
        m11 = _sb(nc, es, "m11", [1, 2], F32)
        mrow = _sb(nc, es, "mrow", [1, 8], F32)
        mrow2 = _sb(nc, es, "mrow2", [1, 8], F32)
        sct = _sb(nc, es, "sct", [1, 3], F32)
        is2_bc = _sb(nc, es, "is2_bc", [128, 1], F32)
        isu_bc = _sb(nc, es, "isu_bc", [128, 1], F32)
        ssu_bc = _sb(nc, es, "ssu_bc", [128, 1], F32)

        s_m8 = nc.semaphore("s_m8").__enter__()
        s_scw = nc.semaphore("s_scw").__enter__()
        s_bc = nc.semaphore("s_bc").__enter__()
        s_fin = nc.semaphore("s_fin").__enter__()
        r_tld = Ring(nc, "r_tld", 3)
        r_uld = Ring(nc, "r_uld", 3)
        r_zo = Ring(nc, "r_zo", 3)
        V = Eng(nc, "cV")
        A = Eng(nc, "cA")
        G = Eng(nc, "cG")
        SY = Eng(nc, "cS")
        M = {}

        def body_v(E):
            E.wait(s_m8, 32)
            E.do(lambda: E.h.tensor_reduce(out=m11[0:1, 0:1],
                                           in_=mrow[0:1, 0:8], axis=AX.X,
                                           op=ALU.max))
            E.do(lambda: E.h.tensor_reduce(out=m11[0:1, 1:2],
                                           in_=mrow2[0:1, 0:8], axis=AX.X,
                                           op=ALU.max))
            E.do(lambda: E.h.tensor_scalar(out=acc1[0:1, 0:1],
                                           in0=m11[0:1, 1:2],
                                           scalar1=1.0 / QMAX,
                                           scalar2=SCALE_FLOOR,
                                           op0=ALU.mult, op1=ALU.max))
            E.do(lambda: E.h.tensor_scalar(out=acc2[0:1, 0:1],
                                           in0=m11[0:1, 0:1],
                                           scalar1=1.0 / QMAX,
                                           scalar2=SCALE_FLOOR,
                                           op0=ALU.mult, op1=ALU.max))
            E.do(lambda: E.h.reciprocal(out=sct[0:1, 0:1],
                                        in_=acc1[0:1, 0:1]))
            E.do(lambda: E.h.reciprocal(out=sct[0:1, 1:2],
                                        in_=acc2[0:1, 0:1]))
            M["sct"] = E.do(lambda: E.h.tensor_tensor(
                out=sct[0:1, 2:3], in0=acc1[0:1, 0:1], in1=acc2[0:1, 0:1],
                op=ALU.mult))
            E.wait(s_bc, 48)
            E.do(lambda: E.h.memset(mx_z[:], NEG_BIG))
            E.do(lambda: E.h.memset(mn_z[:], POS_BIG))
            for k in range(NT34):
                E.rwait(r_tld, k)
                M[("g2", k)] = E.do(lambda k=k: E.h.tensor_scalar(
                    out=g2[k % 2][:], in0=tt[k % 3][:],
                    scalar1=is2_bc[:, 0:1], scalar2=None, op0=ALU.mult))
                E.rwait(r_uld, k)
                M[("ui", k)] = E.do(lambda k=k: E.h.tensor_scalar(
                    out=ui[k % 2][:], in0=ut[k % 3][:],
                    scalar1=isu_bc[:, 0:1], scalar2=None, op0=ALU.mult))
                if k >= 3:
                    E.rwait(r_zo, k - 3)
                M[("z", k)] = E.do(lambda k=k: E.h.scalar_tensor_tensor(
                    out=zz[k % 3][:], in0=g2[k % 2][:],
                    scalar=ssu_bc[:, 0:1], in1=ui[k % 2][:],
                    op0=ALU.mult, op1=ALU.mult))
                E.do(lambda k=k: E.h.tensor_scalar(
                    out=scrm[:], in0=zz[k % 3][:], scalar1=1.0, scalar2=None,
                    op0=ALU.mult, op1=ALU.max, accum_out=wzmx[:, k:k + 1]))
                E.do(lambda k=k: E.h.tensor_scalar(
                    out=scrm[:], in0=zz[k % 3][:], scalar1=1.0, scalar2=None,
                    op0=ALU.mult, op1=ALU.min, accum_out=wzmn[:, k:k + 1]))
            E.do(lambda: E.h.tensor_reduce(out=mx_z[:, 0:1],
                                           in_=wzmx[:, 0:NT34], axis=AX.X,
                                           op=ALU.max))
            E.do(lambda: E.h.tensor_reduce(out=mn_z[:, 0:1],
                                           in_=wzmn[:, 0:NT34], axis=AX.X,
                                           op=ALU.min))
            M["col"] = _absmax_col(E, mx_z, mn_z, col_scr[:, 0:1])

        def body_a(E):
            for k in range(NT34):
                Icl, half = divmod(k, 2)
                if k >= 3:
                    E.weng(V, M.get(("ui", k - 3), 0))
                E.dma(lambda Icl=Icl, half=half, k=k: E.h.dma_start(
                    out=ut[k % 3][:],
                    in_=up_in[128 * Icl:128 * (Icl + 1),
                              CW * half:CW * (half + 1)]),
                    ring=r_uld, item=k)

        def body_g(E):
            E.weng(V, M.get("sct", 0))
            E.dma(lambda: E.h.dma_start(out=sc_scr.ap(), in_=sct[0:1, 0:3]),
                  sem=s_scw)
            E.weng(V, M.get("col", 0))
            E.do(lambda: E.h.tensor_reduce(out=m11[0:1, 0:1],
                                           in_=col_scr[:, 0:1], axis=AX.C,
                                           op=ALU.max))
            E.dma(lambda: E.h.dma_start(out=mz_out, in_=m11[0:1, 0:1]),
                  sem=s_fin)

        def body_sync(E):
            E.dma(lambda: E.h.dma_start(
                out=mrow[0:1, 0:8],
                in_=mupg8_in[:, 0:1].rearrange("a b -> b a"),
                allow_slow_non_contiguous=True), sem=s_m8)
            E.dma(lambda: E.h.dma_start(
                out=mrow2[0:1, 0:8], in_=mt8_in.rearrange("a b -> b a")),
                sem=s_m8)
            E.wait(s_scw, 16)
            for i, bc in enumerate((is2_bc, isu_bc, ssu_bc)):
                E.dma(lambda i=i, bc=bc: E.h.dma_start(
                    out=bc[:], in_=sc_scr.ap()[0:1, i:i + 1]
                    .partition_broadcast(128).squeeze(1)), sem=s_bc)
            for k in range(NT34 + 1):
                if k < NT34:
                    Icl, half = divmod(k, 2)
                    if k >= 3:
                        E.weng(V, M.get(("g2", k - 3), 0))
                    E.dma(lambda Icl=Icl, half=half, k=k: E.h.dma_start(
                        out=tt[k % 3][:],
                        in_=t_in[128 * Icl:128 * (Icl + 1),
                                 CW * half:CW * (half + 1)]),
                        ring=r_tld, item=k)
                if k >= 1:
                    j = k - 1
                    Icl, half = divmod(j, 2)
                    E.weng(V, M.get(("z", j), 0))
                    E.dma(lambda Icl=Icl, half=half, j=j: E.h.dma_start(
                        out=z_out[128 * Icl:128 * (Icl + 1),
                                  CW * half:CW * (half + 1)],
                        in_=zz[j % 3][:]), ring=r_zo, item=j)
            E.rwait_all(r_zo, NT34)
            E.wait(s_fin, 16)

        _two_pass(nc, [("vector", V, body_v), ("scalar", A, body_a),
                       ("gpsimd", G, body_g), ("sync", SY, body_sync)])
    return nc



def build_L5():
    nc = bass.Bass("TRN2", target_bir_lowering=False, debug=False)
    z_in = nc.dram_tensor("z16", [I, TOK], F16, kind="ExternalInput").ap()
    xT_in = nc.dram_tensor("xT", [D, TOK], F32, kind="ExternalInput").ap()
    s_in = nc.dram_tensor("s16", [TOK, D], F16, kind="ExternalInput").ap()
    wdq_in = nc.dram_tensor("wdq", [D, I], F16, kind="ExternalInput").ap()
    mz8_in = nc.dram_tensor("mz8", [8, 1], F32, kind="ExternalInput").ap()
    out_out = nc.dram_tensor("outfm", [D, TOK], F32,
                             kind="ExternalOutput").ap()
    so_scr = nc.dram_tensor("so_scr", [1, 2], F32)

    NTC = TOK // 512
    NJ = NTC * HC

    es = ExitStack()
    with es:
        ot = [_sb(nc, es, f"ot{i}", [128, IC * 512], F16) for i in range(2)]
        oi = [_sb(nc, es, f"oi{i}", [128, 512], I16) for i in range(2)]
        zt = [_sb(nc, es, f"zt{i}", [128, 512], F16) for i in range(3)]
        wdl = _sb(nc, es, "wdl", [128, HC * IC * 128], F16)
        gsil = [_sb(nc, es, f"gsil{i}", [128, 512], F32) for i in range(16)]
        xft = [_sb(nc, es, f"xft{i}", [128, 512], F32) for i in range(6)]
        sft = [_sb(nc, es, f"sft{i}", [128, 512], F16) for i in range(6)]
        ost = [_sb(nc, es, f"ost{i}", [128, 512], F32) for i in range(4)]
        pss = [es.enter_context(nc.psum_tensor(f"ps{i}", [128, 512], F32))
               for i in range(8)]
        mrow = _sb(nc, es, "mrow", [1, 8], F32)
        m11 = _sb(nc, es, "m11", [1, 1], F32)
        sot = _sb(nc, es, "sot", [1, 2], F32)
        iso_bc = _sb(nc, es, "iso_bc", [128, 1], F32)
        so_bc = _sb(nc, es, "so_bc", [128, 1], F32)

        s_mz8 = nc.semaphore("s_mz8").__enter__()
        s_sow = nc.semaphore("s_sow").__enter__()
        s_bc = nc.semaphore("s_bc").__enter__()
        s_wdl = nc.semaphore("s_wdl").__enter__()
        r_zld = Ring(nc, "r_zld", 3)
        r_xld = Ring(nc, "r_xld", 6)
        r_sld = Ring(nc, "r_sld", 6)
        r_out = Ring(nc, "r_out", 4)
        V = Eng(nc, "cV")
        A = Eng(nc, "cA")
        G = Eng(nc, "cG")
        Tn = Eng(nc, "cT")
        SY = Eng(nc, "cS")
        M = {}

        def body_v(E):
            E.wait(s_mz8, 16)
            E.do(lambda: E.h.tensor_reduce(out=m11[0:1, 0:1],
                                           in_=mrow[0:1, 0:8], axis=AX.X,
                                           op=ALU.max))
            E.do(lambda: E.h.tensor_scalar(out=sot[0:1, 0:1],
                                           in0=m11[0:1, 0:1],
                                           scalar1=1.0 / QMAX,
                                           scalar2=SCALE_FLOOR,
                                           op0=ALU.mult, op1=ALU.max))
            M["sot"] = E.do(lambda: E.h.reciprocal(out=sot[0:1, 1:2],
                                                   in_=sot[0:1, 0:1]))
            E.wait(s_bc, 32)
            for tc in range(NTC + 1):
                if tc < NTC:
                    for Icl in range(IC):
                        k = tc * IC + Icl
                        E.rwait(r_zld, k)
                        if Icl == 0 and tc >= 2:
                            E.weng(Tn, M.get(("mm", 8 * (tc - 1) - 1), 0))
                        E.do(lambda k=k: E.h.tensor_scalar(
                            out=oi[k % 2][:], in0=zt[k % 3][:],
                            scalar1=iso_bc[:, 0:1], scalar2=None,
                            op0=ALU.mult))
                        M[("oc", k)] = E.do(
                            lambda tc=tc, Icl=Icl, k=k: E.h.tensor_copy(
                                out=ot[tc % 2][:,
                                               512 * Icl:512 * (Icl + 1)],
                                in_=oi[k % 2][:]))
                if tc >= 1:
                    tcj = tc - 1
                    for hc in range(HC):
                        j = tcj * HC + hc
                        E.weng(A, M.get(("sil", j), 0))
                        E.rwait(r_sld, j)
                        E.rwait(r_xld, j)
                        if j >= 4:
                            E.rwait(r_out, j - 4)
                        E.do(lambda j=j: E.h.tensor_tensor(
                            out=ost[j % 4][:], in0=sft[j % 6][:],
                            in1=gsil[j % 16][:], op=ALU.mult))
                        M[("cmb", j)] = E.do(lambda j=j: E.h.tensor_tensor(
                            out=ost[j % 4][:], in0=ost[j % 4][:],
                            in1=xft[j % 6][:], op=ALU.add))

        def body_t(E):
            E.wait(s_wdl, 16 * HC * IC)
            for tc in range(NTC):
                E.weng(V, M.get(("oc", (tc + 1) * IC - 1), 0))
                for hc in range(HC):
                    j = tc * HC + hc
                    bank = pss[j % 8]
                    if j >= 8:
                        E.weng(A, M.get(("sil", j - 8), 0))
                    for Icl in range(IC):
                        def mk(Icl=Icl, hc=hc, tc=tc, bank=bank):
                            return E.h.matmul(
                                bank[:],
                                wdl[:, 128 * (IC * hc + Icl):
                                    128 * (IC * hc + Icl) + 128],
                                ot[tc % 2][:, 512 * Icl:512 * (Icl + 1)],
                                start=(Icl == 0), stop=(Icl == IC - 1))
                        if Icl < IC - 1:
                            E.raw(mk)
                        else:
                            M[("mm", j)] = E.do(mk, chain=False)

        def body_a(E):
            for j in range(NJ):
                tc, hc = divmod(j, HC)
                if j >= 6:
                    E.weng(V, M.get(("cmb", j - 6), 0))
                E.dma(lambda tc=tc, hc=hc, j=j: E.h.dma_start_transpose(
                    sft[j % 6][:],
                    s_in[512 * tc:512 * (tc + 1),
                         128 * hc:128 * (hc + 1)]), ring=r_sld, item=j)
                E.weng(Tn, M.get(("mm", j), 0))
                if j >= 16:
                    E.weng(V, M.get(("cmb", j - 16), 0))
                M[("sil", j)] = E.do(lambda j=j: E.h.activation(
                    out=gsil[j % 16][:], in_=pss[j % 8][:], func=AF.Silu,
                    scale=so_bc[:, 0:1]))

        def body_g(E):
            E.weng(V, M.get("sot", 0))
            E.dma(lambda: E.h.dma_start(out=so_scr.ap(), in_=sot[0:1, 0:2]),
                  sem=s_sow)
            for j in range(NJ + 1):
                if j < NJ:
                    tc, hc = divmod(j, HC)
                    if j >= 6:
                        E.weng(V, M.get(("cmb", j - 6), 0))
                    E.dma(lambda tc=tc, hc=hc, j=j: E.h.dma_start(
                        out=xft[j % 6][:],
                        in_=xT_in[128 * hc:128 * (hc + 1),
                                  512 * tc:512 * (tc + 1)]),
                        ring=r_xld, item=j)
                if j >= 1:
                    jj = j - 1
                    tc, hc = divmod(jj, HC)
                    E.weng(V, M.get(("cmb", jj), 0))
                    E.dma(lambda tc=tc, hc=hc, jj=jj: E.h.dma_start(
                        out=out_out[128 * hc:128 * (hc + 1),
                                    512 * tc:512 * (tc + 1)],
                        in_=ost[jj % 4][:]), ring=r_out, item=jj)

        def body_sync(E):
            E.dma(lambda: E.h.dma_start(
                out=mrow[0:1, 0:8], in_=mz8_in.rearrange("a b -> b a")),
                sem=s_mz8)
            E.wait(s_sow, 16)
            for i, bc in enumerate((so_bc, iso_bc)):
                E.dma(lambda i=i, bc=bc: E.h.dma_start(
                    out=bc[:], in_=so_scr.ap()[0:1, i:i + 1]
                    .partition_broadcast(128).squeeze(1)), sem=s_bc)
            for hc in range(HC):
                for Icl in range(IC):
                    E.dma(lambda hc=hc, Icl=Icl: E.h.dma_start_transpose(
                        wdl[:, 128 * (IC * hc + Icl):
                            128 * (IC * hc + Icl) + 128],
                        wdq_in[128 * hc:128 * (hc + 1),
                               128 * Icl:128 * (Icl + 1)]), sem=s_wdl)
            for k in range(NTC * IC):
                tc, Icl = divmod(k, IC)
                if k >= 3:
                    E.weng(V, M.get(("oc", k - 3), 0))
                E.dma(lambda tc=tc, Icl=Icl, k=k: E.h.dma_start(
                    out=zt[k % 3][:],
                    in_=z_in[128 * Icl:128 * (Icl + 1),
                             512 * tc:512 * (tc + 1)]), ring=r_zld, item=k)
            E.rwait_all(r_out, NJ)

        _two_pass(nc, [("vector", V, body_v), ("scalar", A, body_a),
                       ("tensor", Tn, body_t), ("gpsimd", G, body_g),
                       ("sync", SY, body_sync)])
    return nc



_CACHE = {}


def _get(name, fn):
    if name not in _CACHE:
        _CACHE[name] = fn()
    return _CACHE[name]


def kernel(L0_attn_u, L0_attn_diff_kernel, L0_attn_ma_r_kernel,
           L0_ffn_w_gate, L0_ffn_w_up, L0_ffn_w_down):
    cores = list(range(NCORES))
    x = np.ascontiguousarray(np.asarray(L0_attn_u, np.float32))
    wg = np.ascontiguousarray(np.asarray(L0_ffn_w_gate, np.float32))
    wu = np.ascontiguousarray(np.asarray(L0_ffn_w_up, np.float32))
    wd = np.ascontiguousarray(np.asarray(L0_ffn_w_down, np.float32))
    toe_d, toe_m = _toeplitz_host(L0_attn_diff_kernel, L0_attn_ma_r_kernel)

    xs = [np.ascontiguousarray(x[BPC * c:BPC * (c + 1)].reshape(TOK, D))
          for c in range(NCORES)]
    xTs = [np.ascontiguousarray(s.T) for s in xs]

    r1 = run_bass_kernel_spmd(
        _get("L1", build_L1),
        [{"x": xs[c], "wup": wu, "wg": wg} for c in cores], core_ids=cores)
    mx8 = np.ascontiguousarray(
        np.stack([r1.results[c]["mx1"][0] for c in cores]))

    r2 = run_bass_kernel_spmd(
        _get("L2", build_L2),
        [{"xn16": r1.results[c]["xn16"], "wupq": r1.results[c]["wupq"],
          "wgq": r1.results[c]["wgq"], "toe_d": toe_d, "toe_m": toe_m,
          "wd": wd, "mx8": mx8} for c in cores], core_ids=cores)
    mupg8 = np.ascontiguousarray(
        np.stack([r2.results[c]["mupg"][0] for c in cores]))

    r3 = run_bass_kernel_spmd(
        _get("L3", build_L3),
        [{"g16": r2.results[c]["g16"], "mupg8": mupg8} for c in cores],
        core_ids=cores)
    mt8 = np.ascontiguousarray(
        np.stack([r3.results[c]["mt1"][0] for c in cores]))

    r4 = run_bass_kernel_spmd(
        _get("L4", build_L4),
        [{"t16": r3.results[c]["t16"], "up16": r2.results[c]["up16"],
          "mupg8": mupg8, "mt8": mt8} for c in cores], core_ids=cores)
    mz8 = np.ascontiguousarray(
        np.stack([r4.results[c]["mz1"][0] for c in cores]))

    r5 = run_bass_kernel_spmd(
        _get("L5", build_L5),
        [{"z16": r4.results[c]["z16"], "xT": xTs[c],
          "s16": r2.results[c]["s16"], "wdq": r2.results[c]["wdq"],
          "mz8": mz8} for c in cores], core_ids=cores)

    out = np.empty((B, T, D), np.float32)
    for c in cores:
        ofm = np.asarray(r5.results[c]["outfm"])
        out[BPC * c:BPC * (c + 1)] = ofm.reshape(D, BPC, T).transpose(1, 2, 0)
    return out
